# version 09
# baseline (speedup 1.0000x reference)
import sys
sys.path.insert(0, '/opt/trn_rl_repo')
import numpy as np
import concourse.bass as bass
import concourse.mybir as mybir
from concourse.bass_utils import run_bass_kernel_spmd

F32 = mybir.dt.float32
BF16 = mybir.dt.bfloat16
AF = mybir.ActivationFunctionType
ALU = mybir.AluOpType
AX = mybir.AxisListType

D, DI, DS, DTR, DFF, T = 1024, 2048, 16, 64, 4096, 4096
ND, NDT, NF = 8, 16, 32
TC1, TC4, TCB = 1024, 256, 512
NC1, NC4, NCB = T // TC1, T // TC4, T // TCB

DEBUG = False
DEV_NCHUNKS = 4
DEV_STAGE = DEV_SUB = DEV_C2SUB = 9
DEV_NOEVAC = DEV_NOMM = 0
PHASES = ("P0", "A12", "A4", "B1", "B2", "P4A", "P4B", "P4C", "P4D")


class Ctx:
    COMPUTE = ("VV", "SC", "GP", "MM", "CC")

    def __init__(self, nc):
        self.nc = nc
        self.ops = {k: [] for k in ("sync", "vector", "scalar", "tensor", "gpsimd")}
        self.sem = {}
        for name in self.COMPUTE:
            self.sem[name] = [nc.alloc_semaphore(name), 0]
        self.dma_sems = []

    def dsem(self, name):
        if name not in self.sem:
            self.sem[name] = [self.nc.alloc_semaphore(name), 0]
            self.dma_sems.append(name)
        return name

    def em(self, eng, fn, inc=None, n=1):
        if inc is None:
            self.ops[eng].append(fn)
        else:
            s = self.sem[inc]
            s[1] += n
            self.ops[eng].append(lambda e, f=fn, h=s[0], k=n: f(e).then_inc(h, k))

    def wait(self, eng, sem, v=None):
        s = self.sem[sem]
        t = s[1] if v is None else v
        if t > 0:
            self.ops[eng].append(lambda e, h=s[0], tv=t: e.wait_ge(h, tv))

    def v(self, sem):
        return self.sem[sem][1]

    def barrier(self):
        self.wait("tensor", "VV")
        self.em("vector", self.vec_inc_op, "VV")
        self.em("scalar", self.sc_inc_op, "SC")
        self.em("tensor", self.mm_inc_op, "MM")
        for sem in ("VV", "SC", "MM"):
            self.wait("gpsimd", sem)
        for sem in self.dma_sems:
            self.wait("gpsimd", sem)
        self.em("gpsimd", lambda e: e.nop(), "GP")
        gp = self.v("GP")
        for eng in ("sync", "vector", "scalar", "tensor"):
            self.wait(eng, "GP", gp)

    def load(self, dst, src, sem="L0"):
        self.dsem(sem)
        self.em("sync", lambda e, d=dst, s=src: e.dma_start(out=d, in_=s), sem, 16)

    def store(self, dst, src, sem="SP"):
        self.dsem(sem)
        self.em("scalar", lambda e, d=dst, s=src: e.dma_start(out=d, in_=s), sem, 16)


class View:
    def __init__(self, t):
        self.t, self.off = t, 0

    def take(self, cols, p=128):
        ap = self.t[0:p, self.off:self.off + cols]
        self.off += cols
        return ap


def nsplits(total, cap=512):
    out, o = [], 0
    while o < total:
        w = min(cap, total - o)
        out.append((o, w))
        o += w
    return out


def build_nc():
    nc = bass.Bass("TRN2", target_bir_lowering=False, detect_race_conditions=False)
    c = Ctx(nc)
    em, wait, load, store = c.em, c.wait, c.load, c.store

    def ein(name, shape, dt=F32):
        return nc.dram_tensor(name, shape, dt, kind="ExternalInput").ap()

    x_in = ein("x", [D, T])
    xhalo_in = ein("xhalo", [D, 3])
    w_inproj = ein("w_inproj", [D, 2 * DI])
    w_xproj = ein("w_xproj", [DI, 96])
    w_dtproj = ein("w_dtproj", [DTR, DI])
    w_outproj = ein("w_outproj", [DI, D])
    w_ffn1 = ein("w_ffn1", [D, DFF])
    w_ffn2 = ein("w_ffn2", [DFF, D])
    w_bn1 = ein("w_bn1", [D, DFF])
    w_bn2 = ein("w_bn2", [DFF, D])
    w_mod = ein("w_mod", [D, 6 * D])
    ins_small = {}
    for name, cols in [("c_vec", ND), ("ln1g", ND), ("ln1b", ND), ("ln2g", ND),
                       ("ln2b", ND), ("mconv_w", NDT * 4), ("mconv_b", NDT),
                       ("dt_bias", NDT), ("a_log", NDT * DS), ("d_vec", NDT),
                       ("bconv_w", NF * 3), ("bconv_b", NF), ("fb1", NF),
                       ("fb2", ND), ("hmask", 3), ("consts", 8), ("consts2", 16)]:
        ins_small[name] = ein(name, [128, cols])
    ident_in = ein("ident_in", [128, 128])
    edge_in = None
    x2_in = None

    out_ext = nc.dram_tensor("out", [D, T], F32, kind="ExternalOutput").ap()

    def scratch(name, shape, dt=F32):
        if DEBUG:
            return nc.dram_tensor("dbg_" + name, shape, dt, kind="ExternalOutput").ap()
        return nc.dram_tensor("spx_" + name, shape, dt).ap()

    sp_h1 = scratch("h1", [D, T])
    sp_xc = scratch("xc", [DI, T])
    sp_z = scratch("z", [DI, T])
    sp_xdbl = scratch("xdbl", [96, T])
    sp_mod = scratch("mod", [128, 48])
    sp_stat = scratch("stat", [1, 3300])
    sp_cl = scratch("cl", [128, 64])
    sp_cum = scratch("cum", [DI, T])
    sp_yl = scratch("yl", [DI, T])
    sp_state = scratch("state", [128, NDT * DS])
    sp_hh = scratch("hh", [128, DS * (TC4 + 1)])
    sp_gated = scratch("gated", [DI, T])
    sp_a1 = scratch("a1", [D, T])
    sp_ln2n = scratch("ln2n", [D, T])
    sp_mid = scratch("mid", [DFF, T])
    sp_x2 = scratch("x2", [D, T])
    sp_w2t = scratch("w2t", [128, NF * 1024], BF16)
    sp_bmid2 = scratch("bmid2", [DFF, T])
    sp_rm2 = scratch("rm2", [1, T + 8])
    dbg_w1t = (nc.dram_tensor("dbg_w1t", [128, ND * 4096], BF16,
                              kind="ExternalOutput").ap() if DEBUG else None)
    ag2_in = nc.dram_tensor("ag2_in", [1, 128 * 16], F32).ap()
    ag2_out = nc.dram_tensor("ag2_out", [2, 128 * 16], F32).ap()
    ag1_in = nc.dram_tensor("ag1_in", [1, 128 * NDT * DS], F32).ap()
    ag1_out = nc.dram_tensor("ag1_out", [2, 128 * NDT * DS], F32).ap()
    sp_row = nc.dram_tensor("sp_row", [16, 1088], F32).ap()
    _ring = [0]

    def bcast(dst, src_row, W):
        i = _ring[0] % 16
        _ring[0] += 1
        store(sp_row[i:i + 1, 0:W], src_row, sem="SR")
        wait("sync", "SR")
        c.dsem("LB")
        c.em("sync", lambda e, d=dst, s=sp_row[i:i + 1, 0:W]: e.dma_start(
            out=d, in_=s.partition_broadcast(128).squeeze(1)), "LB", 16)

    ident = nc.alloc_sbuf_tensor("ident", [128, 128], F32)
    sbp = {k: nc.alloc_sbuf_tensor("p_" + k, [128, v.shape[1]], F32)
           for k, v in ins_small.items()}
    consts = sbp["consts"]
    nc.const_aps.register(F32, 1.0, consts[:, 0:1])
    nc.const_aps.register(F32, -1.0, consts[:, 1:2])
    nc.const_aps.register(F32, 1e-5, consts[:, 2:3])
    ones_col = consts[:, 0:1]
    p_A = nc.alloc_sbuf_tensor("p_A", [128, NDT * DS], F32)
    modv = nc.alloc_sbuf_tensor("modv", [128, 48], F32)
    xdbl = nc.alloc_sbuf_tensor("xdbl", [96, T], F32)
    xpw = nc.alloc_sbuf_tensor("xpw", [128, NDT * 96], F32)
    dtpw = nc.alloc_sbuf_tensor("dtpw", [64, 2048], F32)
    scan_state = nc.alloc_sbuf_tensor("scan_state", [128, NDT * DS], F32)
    cum_state = nc.alloc_sbuf_tensor("cum_state", [128, NDT], F32)
    conv_tails = nc.alloc_sbuf_tensor("conv_tails", [128, NDT * 3], F32)
    edge = nc.alloc_sbuf_tensor("edge", [128, 20], F32)
    h0_sb = nc.alloc_sbuf_tensor("h0_sb", [128, NDT * DS], F32)
    h0g = nc.alloc_sbuf_tensor("h0g", [128, NDT * DS], F32)

    arA = nc.alloc_sbuf_tensor("arA", [128, 16448], F32)
    arB = nc.alloc_sbuf_tensor("arB", [128, 8224], F32)
    arC = nc.alloc_sbuf_tensor("arC", [128, 8224], F32)
    arD = nc.alloc_sbuf_tensor("arD", [128, 4160], F32)
    arE = nc.alloc_sbuf_tensor("arE", [128, 3264], F32)
    arF = nc.alloc_sbuf_tensor("arF", [128, 3264], F32)
    stat = arF[0:1, 0:3264]
    rem = nc.sbuf_bytes_remaining
    print("sbuf remaining bytes:", rem)
    assert rem >= 0

    pb = [nc.alloc_psum_tensor(f"pb{i}", [128, 512], F32) for i in range(8)]
    c.vec_inc_op = lambda e: e.tensor_copy(consts[:, 7:8], consts[:, 6:7])
    c.sc_inc_op = lambda e: e.copy(consts[:, 7:8], consts[:, 6:7])
    c.mm_inc_op = lambda e: e.matmul(pb[7][0:1, 0:1], consts[:, 0:1], consts[:, 0:1],
                                     start=True, stop=True)

    def p0():
        for k, src in ins_small.items():
            load(sbp[k][:], src)
        load(ident[:], ident_in)
        load(xpw[:].rearrange("p (k n) -> p k n", n=96), w_xproj.rearrange("(k p) n -> p k n", p=128))
        load(dtpw[:], w_dtproj)
        ld0 = c.v("L0")
        wait("scalar", "L0", ld0)
        em("scalar", lambda e: e.activation(p_A[:], sbp["a_log"][:], AF.Exp))
        em("scalar", lambda e: e.mul(p_A[:], p_A[:], -1.0), "SC")
        vE = View(arE)
        silu_c = vE.take(ND)
        em("scalar", lambda e, o=silu_c: e.activation(o, sbp["c_vec"][:], AF.Silu), "SC")
        sc0 = c.v("SC")
        wt = [vE.take(128) for _ in range(4)]
        mm_at = {}
        ev_at = {}
        for m in range(48):
            if m >= 2:
                wait("tensor", "VV", ev_at[m - 2])
            for k in range(ND):
                i = m * ND + k
                buf = wt[i % 4]
                lw = "LW%d" % (i % 4)
                if i >= 4:
                    wait("sync", "MM", mm_at[i - 4])
                load(buf, w_mod[k * 128:(k + 1) * 128, m * 128:(m + 1) * 128], sem=lw)
                wait("tensor", lw)
                if i == 0:
                    wait("tensor", "SC", sc0)
                em("tensor", lambda e, b=buf, kk=k, m=m: e.matmul(
                    pb[m % 2][:, 0:1], b, silu_c[:, kk:kk + 1],
                    start=(kk == 0), stop=(kk == ND - 1)), "MM")
                mm_at[i] = c.v("MM")
            wait("vector", "MM", mm_at[m * ND + ND - 1])
            em("vector", lambda e, m_=m: e.tensor_copy(modv[:, m_:m_ + 1], pb[m_ % 2][:, 0:1]), "VV")
            ev_at[m] = c.v("VV")
        ones8 = sbp["consts2"][:, 0:8]
        q8 = sbp["consts2"][:, 8:16]
        bufsets = [[vE.take(ND) for _ in range(7)] for _ in range(2)]
        for bi, base in enumerate((16, 40)):
            g = modv[:, base:base + ND]
            b1, b2, b3, b4, b5, b6, b7 = bufsets[bi]
            wait("scalar", "VV")
            em("scalar", lambda e, g=g, o=b1: e.activation(o, g, AF.Sign))
            em("scalar", lambda e, i=b1, o=b2: e.activation(o, i, AF.Square), "SC")
            wait("vector", "SC")
            em("vector", lambda e, a=b1, b=b2, o=b3: e.tensor_sub(o, a, b), "VV")
            wait("vector", "VV")
            em("vector", lambda e, i=b3, o=b4: e.tensor_add(o, i, ones8), "VV")
            wait("vector", "VV")
            em("vector", lambda e, g=g, s=b4, o=b5: e.tensor_mul(o, g, s), "VV")
            wait("vector", "VV")
            em("vector", lambda e, i=b5, o=b6: e.tensor_max(o, i, q8), "VV")
            wait("vector", "VV")
            em("vector", lambda e, s=b4, m=b6, o=b7: e.tensor_mul(o, s, m), "VV")
            wait("vector", "VV")
            em("vector", lambda e, o=b7, g=g: e.tensor_copy(g, o), "VV")
            if DEBUG and bi == 0:
                c.barrier()
                for di_, b in enumerate(bufsets[0]):
                    store(sp_cl[:, di_ * 8:di_ * 8 + 8], b)
                store(sp_cl[:, 56:64], ones8)
                c.barrier()
        for base in (8, 32):
            g = modv[:, base:base + ND]
            em("vector", lambda e, g=g: e.tensor_single_scalar(g, g, 1.0, ALU.add), "VV")
        if DEBUG:
            wait("scalar", "VV")
            store(sp_mod, modv[:])

    def emit_ln(in_tiles, out_tiles, W, scale_cols, bias_cols, pbank, sq, rep_a, rep_b):
        n_in = len(in_tiles)
        mmv = {}
        gq = [0]
        srow = stat[0:1, 0:W]
        srow2 = stat[0:1, 1088:1088 + W]
        mrow = stat[0:1, 2176:2176 + W]
        for o, w in nsplits(W):
            wait("tensor", "VV")
            if not DEV_NOMM:
                for i, t in enumerate(in_tiles):
                    em("tensor", lambda e, t=t, o=o, w=w, i=i: e.matmul(
                        pbank[0:1, 0:w], ones_col, t[:, o:o + w],
                        start=(i == 0), stop=(i == n_in - 1)),
                        "MM" if i == n_in - 1 else None)
            if DEV_NOEVAC:
                continue
            wait("vector", "MM")
            em("vector", lambda e, o=o, w=w: e.tensor_copy(srow[0:1, o:o + w], pbank[0:1, 0:w]), "VV")
            if DEV_SUB < 2:
                continue
            wait("tensor", "VV")
            for i, t in enumerate(in_tiles):
                gi = gq[0]
                gq[0] += 1
                s = sq[gi % 2]
                if gi >= 2:
                    wait("scalar", "MM", mmv[gi - 2])
                em("scalar", lambda e, t=t, o=o, w=w, s=s: e.activation(
                    s[:, 0:w], t[:, o:o + w], AF.Square), "SC")
                wait("tensor", "SC")
                em("tensor", lambda e, w=w, i=i, s=s: e.matmul(
                    pbank[0:1, 0:w], ones_col, s[:, 0:w],
                    start=(i == 0), stop=(i == n_in - 1)), "MM")
                mmv[gi] = c.v("MM")
            wait("vector", "MM")
            em("vector", lambda e, o=o, w=w: e.tensor_copy(srow2[0:1, o:o + w], pbank[0:1, 0:w]), "VV")
        if DEV_SUB < 3:
            return
        Dm = float(128 * n_in)
        em("vector", lambda e: e.tensor_single_scalar(srow, srow, 1.0 / Dm, ALU.mult))
        em("vector", lambda e: e.tensor_single_scalar(srow2, srow2, 1.0 / Dm, ALU.mult))
        em("vector", lambda e: e.tensor_mul(mrow, srow, srow))
        em("vector", lambda e: e.tensor_sub(srow2, srow2, mrow), "VV")
        wait("scalar", "VV")
        em("scalar", lambda e: e.activation(srow2, srow2, AF.Sqrt, bias=1e-5), "SC")
        wait("vector", "SC")
        em("vector", lambda e: e.reciprocal(srow2, srow2))
        em("vector", lambda e: e.tensor_mul(mrow, srow, srow2), "VV")
        wait("scalar", "VV")
        bcast(rep_a[:, 0:W], srow2, W)
        bcast(rep_b[:, 0:W], mrow, W)
        wait("vector", "LB")
        if DEV_SUB < 9:
            return
        for i, (ti, to) in enumerate(zip(in_tiles, out_tiles)):
            s_ = scale_cols[i] if scale_cols is not None else None
            b_ = bias_cols[i] if bias_cols is not None else None
            if s_ is None:
                em("vector", lambda e, ti=ti, to=to: e.tensor_mul(to[:, 0:W], ti[:, 0:W], rep_a[:, 0:W]))
                em("vector", lambda e, to=to: e.tensor_sub(to[:, 0:W], to[:, 0:W], rep_b[:, 0:W]), "VV")
            else:
                em("vector", lambda e, s=s_: e.tensor_scalar(
                    sq[0][:, 0:W], rep_a[:, 0:W], s, None, ALU.mult))
                em("vector", lambda e, s=s_, b=b_: e.tensor_scalar(
                    sq[1][:, 0:W], rep_b[:, 0:W], s, b, ALU.mult, ALU.subtract))
                em("vector", lambda e, ti=ti, to=to: e.tensor_mul(to[:, 0:W], ti[:, 0:W], sq[0][:, 0:W]))
                em("vector", lambda e, to=to: e.tensor_sub(to[:, 0:W], to[:, 0:W], sq[1][:, 0:W]), "VV")

    def a12():
        for ci in range(DEV_NCHUNKS):
            if ci == 0 or DEV_C2SUB >= 2:
                c.barrier()
            g0 = ci * TC1
            halo = 3 if ci == 0 else 0
            W = TC1 + halo
            vA, vB, vC, vD, vE, vF = View(arA), View(arB), View(arC), View(arD), View(arE), View(arF)
            xt = [vA.take(W) for _ in range(ND)]
            h1t = [vA.take(W) for _ in range(ND)]
            h1nt = [vC.take(W) for _ in range(ND)]
            sq = [vD.take(W), vD.take(W)]
            rep_a, rep_b = vD.take(W), vD.take(W)
            cin = [vB.take(TC1 + 3), vB.take(TC1 + 3)]
            xc_t = vB.take(TC1)
            zt = vB.take(W)
            wbuf = [vE.take(8 * 128), vE.take(8 * 128)]
            lx = "LX%d" % (ci % 2)
            for i in range(ND):
                if halo:
                    load(xt[i][:, 0:3], xhalo_in[i * 128:(i + 1) * 128, :], sem=lx)
                    load(xt[i][:, 3:W], x_in[i * 128:(i + 1) * 128, g0:g0 + TC1], sem=lx)
                else:
                    load(xt[i], x_in[i * 128:(i + 1) * 128, g0:g0 + TC1], sem=lx)
            ldx = c.v(lx)
            for eng in ("tensor", "scalar", "vector"):
                wait(eng, lx, ldx)
            if ci > 0 and DEV_C2SUB < 1:
                continue
            emit_ln(xt, h1t, W,
                    [modv[:, 8 + i:9 + i] for i in range(ND)],
                    [modv[:, 0 + i:1 + i] for i in range(ND)],
                    pb[7], sq, rep_a, rep_b)
            wait("scalar", "VV")
            if DEV_SUB < 9:
                continue
            if DEBUG and ci == 0:
                for so in (0, 1088, 2176):
                    store(sp_stat[0:1, so:so + 1027], stat[0:1, so:so + 1027])
            for i in range(ND):
                store(sp_h1[i * 128:(i + 1) * 128, g0:g0 + TC1], h1t[i][:, halo:W])
            if DEV_STAGE < 2:
                continue
            emit_ln(h1t, h1nt, W,
                    [sbp["ln1g"][:, i:i + 1] for i in range(ND)],
                    [sbp["ln1b"][:, i:i + 1] for i in range(ND)],
                    pb[7], sq, rep_a, rep_b)
            if DEV_STAGE < 3:
                continue
            rhs_ready = c.v("VV")
            mm_after_f = {}
            evg = []
            for f in range(NF):
                wb = wbuf[f % 2].rearrange("p (k n) -> p k n", k=ND)
                lw = "LW%d" % (f % 2)
                if f >= 2:
                    wait("sync", "MM", mm_after_f[f - 2])
                load(wb, w_inproj.rearrange("(k p) n -> p k n", p=128)[:, :, f * 128:(f + 1) * 128], sem=lw)
                wld = c.v(lw)
                di = f if f < NDT else f - NDT
                first = True
                for o, w in nsplits(W):
                    g = len(evg)
                    bank = pb[g % 2]
                    if g >= 2:
                        wait("tensor", evg[g - 2][0], evg[g - 2][1])
                    for k in range(ND):
                        if first:
                            wait("tensor", lw, wld)
                            wait("tensor", "VV", rhs_ready)
                            first = False
                        em("tensor", lambda e, b=bank, wb=wb, k=k, o=o, w=w, rh=h1nt[k]: e.matmul(
                            b[:, 0:w], wb[:, k, :], rh[:, o:o + w],
                            start=(k == 0), stop=(k == ND - 1)),
                            "MM" if k == ND - 1 else None)
                    if f < NDT:
                        cb = cin[f % 2]
                        dst_o = o if halo else 3 + o
                        wait("scalar", "MM")
                        em("scalar", lambda e, cb=cb, o=dst_o, w=w, b=bank: e.copy(
                            cb[:, o:o + w], b[:, 0:w]), "SC")
                        evg.append(("SC", c.v("SC")))
                    else:
                        wait("scalar", "MM")
                        em("scalar", lambda e, o=o, w=w, b=bank, z=zt: e.activation(
                            z[:, o:o + w], b[:, 0:w], AF.Silu), "SC")
                        evg.append(("SC", c.v("SC")))
                mm_after_f[f] = c.v("MM")
                if f < NDT:
                    cb = cin[f % 2]
                    wait("vector", "SC")
                    if halo:
                        em("vector", lambda e, cb=cb: e.tensor_mul(
                            cb[:, 0:3], cb[:, 0:3], sbp["hmask"][:]), "VV")
                    else:
                        em("vector", lambda e, cb=cb, di=di: e.tensor_copy(
                            cb[:, 0:3], conv_tails[:, di * 3:di * 3 + 3]), "VV")
                    em("vector", lambda e, cb=cb, di=di: e.tensor_copy(
                        conv_tails[:, di * 3:di * 3 + 3], cb[:, TC1:TC1 + 3]), "VV")
                    wait("vector", "SP")
                    wait("vector", "MM")
                    wait("vector", "VV")
                    w0 = sbp["mconv_w"][:, di * 4 + 0:di * 4 + 1]
                    em("vector", lambda e, cb=cb, s=w0, xc=xc_t: e.tensor_scalar(
                        xc[:], cb[:, 0:TC1], s, None, ALU.mult))
                    for j in (1, 2, 3):
                        wj = sbp["mconv_w"][:, di * 4 + j:di * 4 + j + 1]
                        em("vector", lambda e, cb=cb, s=wj, j=j, xc=xc_t: e.scalar_tensor_tensor(
                            xc[:], cb[:, j:j + TC1], s, xc[:], ALU.mult, ALU.add),
                            "VV" if j == 3 else None)
                    wait("scalar", "VV")
                    em("scalar", lambda e, di=di, xc=xc_t: e.activation(
                        xc[:], xc[:], AF.Silu,
                        bias=sbp["mconv_b"][:, di:di + 1]), "SC")
                    wait("scalar", "SC")
                    store(sp_xc[di * 128:(di + 1) * 128, g0:g0 + TC1], xc_t[:])
                    xp_ready = c.v("SC")
                    for oi, (o, w) in enumerate(nsplits(TC1)):
                        wait("tensor", "SC", xp_ready)
                        em("tensor", lambda e, di=di, oi=oi, o=o, w=w, xc=xc_t: e.matmul(
                            pb[2 + oi][0:96, 0:w], xpw[:, di * 96:(di + 1) * 96],
                            xc[:, o:o + w],
                            start=(di == 0), stop=(di == NDT - 1)), "MM")
                    mm_after_f[f] = c.v("MM")
                else:
                    wait("scalar", "SC")
                    store(sp_z[di * 128:(di + 1) * 128, g0:g0 + TC1], zt[:, halo:W])
            wait("vector", "MM")
            for oi, (o, w) in enumerate(nsplits(TC1)):
                em("vector", lambda e, oi=oi, o=o, w=w, g0=g0: e.tensor_copy(
                    xdbl[:, g0 + o:g0 + o + w], pb[2 + oi][0:96, 0:w]), "VV")
        if DEV_STAGE >= 3:
            wait("scalar", "VV")
            store(sp_xdbl[:, 0:DEV_NCHUNKS * TC1], xdbl[:, 0:DEV_NCHUNKS * TC1])

    def a4():
        c.barrier()
        vA, vB, vC, vD, vE, vF = View(arA), View(arB), View(arC), View(arD), View(arE), View(arF)
        SW = DS * (TC4 + 1)
        HW_ = SW // 2
        HC = 8 * TC4
        da = [vA.take(SW), vA.take(SW)]
        bb = [vB.take(SW), vB.take(SW)]
        hh = vC.take(SW)
        C_rep = [vC.take(DS * TC4), vA.take(DS * TC4)]
        B_rep = [vD.take(DS * TC4), vA.take(DS * TC4)]
        zeros = vE.take(TC4 + 1)
        cum_in = [vE.take(TC4 + 1), vE.take(TC4 + 1)]
        cum_out = [vE.take(TC4 + 1), vE.take(TC4 + 1)]
        relu_t = vE.take(TC4)
        sp_ln = vE.take(TC4)
        xc_t = [vE.take(TC4), vE.take(TC4)]
        u_t = [vE.take(TC4), vE.take(TC4)]
        yl_t = vE.take(TC4)
        yl2b = [arF[:, 2688:2688 + TC4], arF[:, 2688 + TC4:2688 + 2 * TC4]]
        em("vector", lambda e: e.memset(zeros[:], 0.0))
        for p in range(2):
            em("vector", lambda e, p=p: e.memset(
                da[p].rearrange("q (s t) -> q s t", t=TC4 + 1)[:, :, 0:1], 0.0))
        em("gpsimd", lambda e: e.memset(cum_state[:], 0.0))
        em("gpsimd", lambda e: e.memset(scan_state[:], 0.0), "GP")
        em("vector", lambda e: e.nop(), "VV")
        wait("gpsimd", "VV")
        wait("vector", "GP")
        NU = NC4 * NDT
        spv, addv, cumv, updv, uv, xdv, dav, bv = {}, {}, {}, {}, {}, {}, {}, {}
        col0v, bc0v = {}, {}
        sl_v, sh_v, glv, ghv, redv, mmv = {}, {}, {}, {}, {}, {}
        spc_v, spy_v, lcv = {}, {}, {}
        lbc_v = {}

        def emit_bc_loads(cj):
            par = cj % 2
            for s in range(DS):
                c.dsem("LBc")
                c.em("sync", lambda e, s=s, cj=cj, par=par: e.dma_start(
                    out=B_rep[par][:, s * TC4:(s + 1) * TC4],
                    in_=sp_xdbl[64 + s:65 + s, cj * TC4:(cj + 1) * TC4].partition_broadcast(128).squeeze(1)),
                    "LBc", 16)
                c.em("sync", lambda e, s=s, cj=cj, par=par: e.dma_start(
                    out=C_rep[par][:, s * TC4:(s + 1) * TC4],
                    in_=sp_xdbl[80 + s:81 + s, cj * TC4:(cj + 1) * TC4].partition_broadcast(128).squeeze(1)),
                    "LBc", 16)
            lbc_v[cj] = c.v("LBc")

        emit_bc_loads(0)
        emit_bc_loads(1)
        hh3 = hh.rearrange("q (s t) -> q s t", t=TC4 + 1)
        for u in range(NU + 6):
            if 0 <= u - 3 < NU:
                n = u - 3
                di = n % NDT
                par = n % 2
                wait("scalar", "GP", addv[n])
                if n >= 2:
                    wait("scalar", "VV", sh_v[n - 2])
                for s in range(DS):
                    em("scalar", lambda e, s=s, di=di, par=par: e.activation(
                        da[par][:, s * (TC4 + 1) + 1:(s + 1) * (TC4 + 1)],
                        cum_in[par][:, 1:TC4 + 1], AF.Exp,
                        scale=p_A[:, di * DS + s:di * DS + s + 1]),
                        "SC" if s == DS - 1 else None)
                dav[n] = c.v("SC")
            if 0 <= u - 1 < NU:
                n = u - 1
                di = n % NDT
                par = n % 2
                wait("gpsimd", "SC", spv[n])
                if n >= 2:
                    wait("gpsimd", "SC", dav[n - 2])
                em("gpsimd", lambda e, par=par: e.tensor_add(
                    cum_in[par][:, 1:TC4 + 1], relu_t[:], sp_ln[:]), "GP")
                addv[n] = c.v("GP")
                em("scalar", lambda e, par=par, di=di: e.copy(
                    cum_in[par][:, 0:1], cum_state[:, di:di + 1]), "SC")
                col0v[n] = c.v("SC")
            if 0 <= u - 2 < NU:
                n = u - 2
                ci, di = n // NDT, n % NDT
                par = n % 2
                wait("vector", "GP", addv[n])
                wait("vector", "SC", col0v[n])
                if n >= 2:
                    wait("vector", "SC", updv[n - 2])
                    wait("vector", "SPc", spc_v[n - 2])
                em("vector", lambda e, par=par: e.tensor_tensor_scan(
                    cum_out[par][:], cum_in[par][:], zeros[:], 0.0, ALU.add, ALU.add), "VV")
                cumv[n] = c.v("VV")
                wait("scalar", "VV", cumv[n])
                em("scalar", lambda e, par=par, di=di: e.copy(
                    cum_state[:, di:di + 1], cum_out[par][:, TC4:TC4 + 1]), "SC")
                updv[n] = c.v("SC")
                wait("vector", lcv[n][0], lcv[n][1])
                if n >= 2:
                    wait("vector", "GP", bv[n - 2])
                em("vector", lambda e, par=par: e.tensor_mul(
                    u_t[par][:], cum_in[par][:, 1:TC4 + 1], xc_t[par][:]), "VV")
                uv[n] = c.v("VV")
                wait("scalar", "VV", cumv[n])
                store(sp_cum[di * 128:(di + 1) * 128, ci * TC4:(ci + 1) * TC4],
                      cum_out[par][:, 1:TC4 + 1], sem="SPc")
                spc_v[n] = c.v("SPc")
            if 0 <= u - 5 < NU:
                n = u - 5
                ci, di = n // NDT, n % NDT
                par = n % 2
                wait("vector", "GP", ghv[n])
                em("vector", lambda e: e.tensor_reduce(
                    yl_t[:], hh3[:, :, 1:TC4 + 1].transpose([0, 2, 1]), AX.X, ALU.add), "VV")
                wait("vector", "VV")
                em("vector", lambda e, par=par: e.tensor_add(
                    yl2b[par][:], yl2b[par][:], yl_t[:]), "VV")
                redv[n] = c.v("VV")
                wait("scalar", "VV", redv[n])
                store(sp_yl[di * 128:(di + 1) * 128, ci * TC4:(ci + 1) * TC4],
                      yl2b[par][:], sem="SPy")
                spy_v[n] = c.v("SPy")
                if di == NDT - 1 and ci + 2 < NC4:
                    wait("sync", "GP", ghv[n])
                    emit_bc_loads(ci + 2)
            if 0 <= u - 3 < NU:
                n = u - 3
                ci = n // NDT
                di = n % NDT
                par = n % 2
                if n >= 2:
                    wait("scalar", "VV", sh_v[n - 2])
                em("scalar", lambda e, par=par, di=di: e.copy(
                    bb[par].rearrange("q (s t) -> q s t", t=TC4 + 1)[:, :, 0:1],
                    scan_state[:, di * DS:(di + 1) * DS].unsqueeze(2)), "SC")
                bc0v[n] = c.v("SC")
                wait("gpsimd", "VV", uv[n])
                wait("gpsimd", "LBc", lbc_v[ci])
                if n >= 2:
                    wait("gpsimd", "VV", sh_v[n - 2])
                em("gpsimd", lambda e, par=par, cpar=ci % 2: e.tensor_tensor(
                    bb[par].rearrange("q (s t) -> q s t", t=TC4 + 1)[:, :, 1:TC4 + 1],
                    u_t[par][:].unsqueeze(1).broadcast_to([128, DS, TC4]),
                    B_rep[cpar][:].rearrange("q (s t) -> q s t", t=TC4),
                    ALU.mult), "GP")
                bv[n] = c.v("GP")
                if n >= 2:
                    wait("vector", "SPy", spy_v[n - 2])
                wait("vector", "VV", uv[n])
                em("vector", lambda e, par=par, di=di: e.tensor_scalar(
                    yl2b[par][:], xc_t[par][:], sbp["d_vec"][:, di:di + 1], None, ALU.mult), "VV")
                xdv[n] = c.v("VV")
            if 0 <= u - 1 < NU:
                n = u - 1
                ci, di = n // NDT, n % NDT
                par = n % 2
                lc = "LC%d" % par
                c.dsem(lc)
                if n >= 2:
                    wait("sync", "VV", xdv[n - 2])
                c.em("sync", lambda e, di=di, ci=ci, xt_=xc_t[par]: e.dma_start(
                    out=xt_, in_=sp_xc[di * 128:(di + 1) * 128, ci * TC4:(ci + 1) * TC4]), lc, 16)
                lcv[n] = (lc, c.v(lc))
            if 0 <= u - 4 < NU:
                n = u - 4
                di = n % NDT
                par = n % 2
                cpar = (n // NDT) % 2
                wait("vector", "SC", dav[n])
                wait("vector", "SC", bc0v[n])
                wait("vector", "GP", bv[n])
                em("vector", lambda e, par=par: e.tensor_tensor_scan(
                    hh[:, 0:HW_], da[par][:, 0:HW_], bb[par][:, 0:HW_],
                    0.0, ALU.mult, ALU.add), "VV")
                sl_v[n] = c.v("VV")
                wait("scalar", "VV", sl_v[n])
                em("scalar", lambda e, di=di: e.copy(
                    scan_state[:, di * DS:di * DS + 8].unsqueeze(2),
                    hh3[:, 0:8, TC4:TC4 + 1]), "SC")
                ulo = c.v("SC")
                wait("gpsimd", "SC", ulo)
                em("gpsimd", lambda e, cpar=cpar: e.tensor_tensor(
                    hh3[:, 0:8, 1:TC4 + 1], hh3[:, 0:8, 1:TC4 + 1],
                    C_rep[cpar][:].rearrange("q (s t) -> q s t", t=TC4)[:, 0:8, :],
                    ALU.mult), "GP")
                glv[n] = c.v("GP")
                em("vector", lambda e, par=par: e.tensor_tensor_scan(
                    hh[:, HW_:SW], da[par][:, HW_:SW], bb[par][:, HW_:SW],
                    0.0, ALU.mult, ALU.add), "VV")
                sh_v[n] = c.v("VV")
                wait("scalar", "VV", sh_v[n])
                em("scalar", lambda e, di=di: e.copy(
                    scan_state[:, di * DS + 8:(di + 1) * DS].unsqueeze(2),
                    hh3[:, 8:16, TC4:TC4 + 1]), "SC")
                uhi = c.v("SC")
                wait("gpsimd", "SC", uhi)
                em("gpsimd", lambda e, cpar=cpar: e.tensor_tensor(
                    hh3[:, 8:16, 1:TC4 + 1], hh3[:, 8:16, 1:TC4 + 1],
                    C_rep[cpar][:].rearrange("q (s t) -> q s t", t=TC4)[:, 8:16, :],
                    ALU.mult), "GP")
                ghv[n] = c.v("GP")
            if u < NU:
                n = u
                ci, di = n // NDT, n % NDT
                par = n % 2
                if n >= 1:
                    wait("tensor", "SC", spv[n - 1])
                em("tensor", lambda e, di=di, ci=ci: e.matmul(
                    pb[4][:, 0:TC4],
                    dtpw[:, di * 128:(di + 1) * 128],
                    xdbl[0:64, ci * TC4:(ci + 1) * TC4], start=True, stop=True), "MM")
                mmv[n] = c.v("MM")
                wait("scalar", "MM", mmv[n])
                if n >= 1:
                    wait("scalar", "GP", addv[n - 1])
                em("scalar", lambda e, di=di: e.activation(
                    sp_ln[:], pb[4][:, 0:TC4], AF.Abs, bias=sbp["dt_bias"][:, di:di + 1]))
                em("scalar", lambda e: e.activation(sp_ln[:], sp_ln[:], AF.Exp, scale=-1.0))
                em("scalar", lambda e: e.activation(sp_ln[:], sp_ln[:], AF.Ln, bias=1.0))
                em("scalar", lambda e, di=di: e.activation(
                    relu_t[:], pb[4][:, 0:TC4], AF.Relu, bias=sbp["dt_bias"][:, di:di + 1]), "SC")
                spv[n] = c.v("SC")
        if DEBUG:
            c.barrier()
            store(sp_state, scan_state[:])

    def b1():
        TCB1 = 512
        NB1 = T // TCB1
        c.barrier()
        c.dsem("GD")
        em("gpsimd", lambda e: e.dma_start(
            out=ag1_in.rearrange("o (p n) -> (o p) n", p=128), in_=scan_state[:]), "GD", 16)
        wait("gpsimd", "GD")
        groups = [[0, 1], [2, 3], [4, 5], [6, 7]]
        em("gpsimd", lambda e, groups=groups: e.collective_compute(
            "AllGather", ALU.bypass, replica_groups=groups,
            ins=[ag1_in.opt()], outs=[ag1_out.opt()]), "CC")
        wait("sync", "CC")
        c.dsem("LH")
        c.em("sync", lambda e: e.dma_start(
            out=h0_sb[:], in_=ag1_out[0:1, :].rearrange("o (p n) -> (o p) n", p=128)), "LH", 16)
        wait("scalar", "LH")
        em("scalar", lambda e: e.activation(h0g[:], h0_sb[:], AF.Copy,
                                            scale=consts[:, 3:4]), "SC")
        h0_ready = c.v("SC")
        HP = 8 * TCB1
        vA, vB, vC, vD, vE = View(arA), View(arB), View(arC), View(arD), View(arE)
        pa4 = [vA.take(HP) for _ in range(4)]
        crep = [vB.take(HP), vB.take(HP), vC.take(HP), vC.take(HP)]
        cum_t = [vD.take(TCB1) for _ in range(3)]
        gat_t = [vD.take(TCB1) for _ in range(2)]
        ylr_t = [vE.take(TCB1), vE.take(TCB1)]
        zr_t = [vE.take(TCB1), vE.take(TCB1)]
        yacc = [vE.take(TCB1), vE.take(TCB1)]
        NU = NB1 * NDT * 2
        pv, h0v, cv, t1v, fv = {}, {}, {}, {}, {}
        ldc, ldy, lbc_v = {}, {}, {}

        def parts(u):
            ci, r = divmod(u, NDT * 2)
            di, hf = divmod(r, 2)
            return ci, di, hf

        def emit_cload(hu):
            ci, hf = hu // 2, hu % 2
            buf = crep[hu % 4]
            if hu >= 4:
                pci, phf = (hu - 4) // 2, (hu - 4) % 2
                w_last = pci * (NDT * 2) + (NDT - 1) * 2 + phf
                wait("sync", "VV", cv[w_last])
            for s in range(8):
                sg = hf * 8 + s
                c.dsem("LBc")
                c.em("sync", lambda e, buf=buf, s=s, sg=sg, ci=ci: e.dma_start(
                    out=buf[:, s * TCB1:(s + 1) * TCB1],
                    in_=sp_xdbl[80 + sg:81 + sg, ci * TCB1:(ci + 1) * TCB1]
                    .partition_broadcast(128).squeeze(1)), "LBc", 16)
            lbc_v[hu] = c.v("LBc")

        wait("gpsimd", "SC", h0_ready)
        wait("scalar", "SC", h0_ready)
        c.dsem("SPg")
        emit_cload(0)
        emit_cload(1)
        for u in range(NU + 4):
            if 0 <= u - 2 < NU:
                w = u - 2
                ci, di, hf = parts(w)
                hu = ci * 2 + hf
                pw = pa4[w % 4]
                cw = crep[hu % 4]
                wait("vector", "GP", h0v[w])
                wait("vector", "LBc", lbc_v[hu])
                em("vector", lambda e, pw=pw, cw=cw: e.tensor_tensor(
                    pw[:, 0:HP], pw[:, 0:HP], cw[:, 0:HP], ALU.mult), "VV")
                cv[w] = c.v("VV")
                if hf == 0:
                    c.dsem("LDy")
                    key2 = ci * NDT + di
                    if key2 >= 2:
                        wait("sync", "VV", fv[2 * (key2 - 2) + 1])
                    load(ylr_t[di % 2], sp_yl[di * 128:(di + 1) * 128, ci * TCB1:(ci + 1) * TCB1], sem="LDy")
                    load(zr_t[di % 2], sp_z[di * 128:(di + 1) * 128, ci * TCB1:(ci + 1) * TCB1], sem="LDy")
                    ldy[(ci, di)] = c.v("LDy")
            if 0 <= u - 2 < NU:
                w = u - 2
                ciw, diw, hfw = parts(w)
                if diw == 2 and hfw == 0:
                    for nxt in ((ciw + 1) * 2, (ciw + 1) * 2 + 1):
                        if nxt < NB1 * 2 and nxt not in lbc_v:
                            emit_cload(nxt)
            if 0 <= u - 3 < NU:
                w = u - 3
                pw = pa4[w % 4]
                H4 = 4 * TCB1
                if w % 2 == 0:
                    wait("gpsimd", "VV", cv[w])
                    em("gpsimd", lambda e, pw=pw, H4=H4: e.tensor_tensor(
                        pw[:, 0:H4], pw[:, 0:H4], pw[:, H4:2 * H4], ALU.add), "GP")
                    t1v[w] = ("GP", c.v("GP"))
                else:
                    wait("vector", "VV", cv[w])
                    em("vector", lambda e, pw=pw, H4=H4: e.tensor_tensor(
                        pw[:, 0:H4], pw[:, 0:H4], pw[:, H4:2 * H4], ALU.add), "VV")
                    t1v[w] = ("VV", c.v("VV"))
            if 0 <= u - 4 < NU:
                w = u - 4
                ci, di, hf = parts(w)
                pw = pa4[w % 4]
                sem_n, sem_v = t1v[w]
                wait("vector", sem_n, sem_v)
                if hf == 0:
                    em("vector", lambda e, pw=pw, ya=yacc[di % 2]: e.tensor_reduce(
                        ya[:], pw.rearrange("q (s t) -> q s t", t=TCB1)[:, 0:4, :]
                        .transpose([0, 2, 1]), AX.X, ALU.add), "VV")
                else:
                    wait("vector", "LDy", ldy[(ci, di)])
                    wait("vector", "SPg")
                    g_ = gat_t[di % 2]
                    em("vector", lambda e, pw=pw, g_=g_: e.tensor_reduce(
                        g_[:], pw.rearrange("q (s t) -> q s t", t=TCB1)[:, 0:4, :]
                        .transpose([0, 2, 1]), AX.X, ALU.add), "VV")
                    wait("vector", "VV")
                    em("vector", lambda e, g_=g_, ya=yacc[di % 2]: e.tensor_add(
                        g_[:], g_[:], ya[:]), "VV")
                    wait("vector", "VV")
                    em("vector", lambda e, g_=g_, yl=ylr_t[di % 2]: e.tensor_add(
                        g_[:], g_[:], yl[:]), "VV")
                    wait("vector", "VV")
                    em("vector", lambda e, g_=g_, zr=zr_t[di % 2]: e.tensor_tensor(
                        g_[:], g_[:], zr[:], ALU.mult), "VV")
                    gv_ = c.v("VV")
                    wait("scalar", "VV", gv_)
                    store(sp_gated[di * 128:(di + 1) * 128, ci * TCB1:(ci + 1) * TCB1],
                          gat_t[di % 2][:], sem="SPg")
                fv[w] = c.v("VV")
            if u < NU:
                ci, di, hf = parts(u)
                j = u % 4
                if hf == 0:
                    c.dsem("LDc")
                    key = ci * NDT + di
                    if key >= 3:
                        wait("sync", "SC", pv.get(2 * (key - 3) + 1, 0))
                    load(cum_t[key % 3], sp_cum[di * 128:(di + 1) * 128, ci * TCB1:(ci + 1) * TCB1], sem="LDc")
                    ldc[key] = c.v("LDc")
                if u >= 4:
                    wait("scalar", "VV", fv[u - 4])
                wait("scalar", "LDc", ldc[ci * NDT + di])
                for s in range(8):
                    sg = hf * 8 + s
                    em("scalar", lambda e, s=s, sg=sg, di=di, j=j, ct=cum_t[(ci * NDT + di) % 3]: e.activation(
                        pa4[j][:, s * TCB1:(s + 1) * TCB1], ct[:],
                        AF.Exp, scale=p_A[:, di * DS + sg:di * DS + sg + 1]),
                        "SC" if s == 7 else None)
                pv[u] = c.v("SC")
            if 0 <= u - 1 < NU:
                w = u - 1
                ci, di, hf = parts(w)
                pw = pa4[w % 4].rearrange("q (s t) -> q s t", t=TCB1)
                base = di * DS + hf * 8
                wait("gpsimd", "SC", pv[w])
                em("gpsimd", lambda e, pw=pw, base=base: e.tensor_tensor(
                    pw, pw,
                    h0g[:, base:base + 8].unsqueeze(2).broadcast_to([128, 8, TCB1]),
                    ALU.mult), "GP")
                h0v[w] = c.v("GP")

    def b2a():
        c.barrier()
        vA = View(arA)
        opw = vA.take(NDT * 1024)
        load(opw.rearrange("p (k m) -> p k m", m=1024),
             w_outproj.rearrange("(k p) m -> p k m", p=128), sem="LW0")
        for ti in range(NCB):
            g0 = ti * TCB
            c.barrier()
            vB, vC, vD, vE = View(arB), View(arC), View(arD), View(arE)
            gt = [vB.take(TCB) for _ in range(NDT)]
            h1t = [vC.take(TCB) for _ in range(ND)]
            a1t = [vC.take(TCB) for _ in range(ND)]
            l2t = [vD.take(TCB) for _ in range(ND)]
            sq = [vE.take(TCB), vE.take(TCB)]
            rep_a, rep_b = vE.take(TCB), vE.take(TCB)
            for k in range(NDT):
                load(gt[k], sp_gated[k * 128:(k + 1) * 128, g0:g0 + TCB], sem="LX0")
            for m in range(ND):
                load(h1t[m], sp_h1[m * 128:(m + 1) * 128, g0:g0 + TCB], sem="LX0")
            lxv = c.v("LX0")
            wait("tensor", "LX0", lxv)
            wait("tensor", "LW0")
            for m in range(ND):
                for k in range(NDT):
                    em("tensor", lambda e, m=m, k=k, g_=gt[k]: e.matmul(
                        pb[m][:, 0:TCB], opw.rearrange("p (k m) -> p k m", m=1024)[:, k, m * 128:(m + 1) * 128],
                        g_[:], start=(k == 0), stop=(k == NDT - 1)),
                        "MM" if k == NDT - 1 else None)
            mmv = c.v("MM")
            wait("vector", "MM", mmv)
            wait("vector", "LX0", lxv)
            for m in range(ND):
                em("vector", lambda e, m=m, a_=a1t[m], h_=h1t[m]: e.tensor_add(a_[:], pb[m][:, 0:TCB], h_[:]), "VV")
            a1v = c.v("VV")
            wait("scalar", "VV", a1v)
            for m in range(ND):
                store(sp_a1[m * 128:(m + 1) * 128, g0:g0 + TCB], a1t[m][:], sem="SPa")
            emit_ln(a1t, l2t, TCB,
                    [sbp["ln2g"][:, i:i + 1] for i in range(ND)],
                    [sbp["ln2b"][:, i:i + 1] for i in range(ND)],
                    pb[7], sq, rep_a, rep_b)
            wait("scalar", "VV")
            for m in range(ND):
                store(sp_ln2n[m * 128:(m + 1) * 128, g0:g0 + TCB], l2t[m][:], sem="SPa")

    def b2b():
        for hf in range(2):
            c.barrier()
            vA = View(arA)
            w1h = vA.take(8 * 2048)
            load(w1h.rearrange("p (k n) -> p k n", k=ND),
                 w_ffn1.rearrange("(k p) n -> p k n", p=128)[:, :, hf * 2048:(hf + 1) * 2048],
                 sem="LW0")
            for ti in range(NCB):
                g0 = ti * TCB
                c.barrier()
                vC, vE = View(arC), View(arE)
                l2t = [vC.take(TCB) for _ in range(ND)]
                mid_t = [vE.take(TCB), vE.take(TCB)]
                for m in range(ND):
                    load(l2t[m], sp_ln2n[m * 128:(m + 1) * 128, g0:g0 + TCB], sem="LX0")
                lxv = c.v("LX0")
                wait("tensor", "LX0", lxv)
                wait("tensor", "LW0")
                evg = []
                for f in range(16):
                    fg = hf * 16 + f
                    bank = pb[f % 2]
                    if f >= 2:
                        wait("tensor", evg[f - 2][0], evg[f - 2][1])
                    for k in range(ND):
                        em("tensor", lambda e, f=f, k=k, bank=bank, w_=w1h, l_=l2t[k]: e.matmul(
                            bank[:, 0:TCB],
                            w_.rearrange("p (k n) -> p k n", k=ND)[:, k, f * 128:(f + 1) * 128],
                            l_[:], start=(k == 0), stop=(k == ND - 1)),
                            "MM" if k == ND - 1 else None)
                    wait("scalar", "MM")
                    if f >= 2:
                        wait("scalar", "SPm")
                    em("scalar", lambda e, f=f, fg=fg, bank=bank, mt=mid_t[f % 2]: e.activation(
                        mt[:], bank[:, 0:TCB], AF.Gelu_apprx_tanh,
                        bias=sbp["fb1"][:, fg:fg + 1]), "SC")
                    evg.append(("SC", c.v("SC")))
                    wait("scalar", "SC")
                    store(sp_mid[fg * 128:(fg + 1) * 128, g0:g0 + TCB],
                          mid_t[f % 2][:], sem="SPm")

    def b2c():
        for ti in range(NCB):
            g0 = ti * TCB
            c.barrier()
            vA, vB, vC, vD = View(arA), View(arB), View(arC), View(arD)
            w2h = [vA.take(NDT * 1024) for _ in range(1)]
            midt = [vB.take(TCB) for _ in range(NDT)]
            a1t = [vC.take(TCB) for _ in range(ND)]
            xt = [vC.take(TCB) for _ in range(ND)]
            x2t = [vD.take(TCB) for _ in range(ND)]
            for m in range(ND):
                load(a1t[m], sp_a1[m * 128:(m + 1) * 128, g0:g0 + TCB], sem="LX0")
                load(xt[m], x_in[m * 128:(m + 1) * 128, g0:g0 + TCB], sem="LX0")
            lax = c.v("LX0")
            for hf in range(2):
                load(w2h[0].rearrange("p (k m) -> p k m", m=1024),
                     w_ffn2.rearrange("(k p) m -> p k m", p=128)[:, hf * 16:(hf + 1) * 16, :],
                     sem="LW%d" % hf)
                for k in range(NDT):
                    load(midt[k], sp_mid[(hf * 16 + k) * 128:(hf * 16 + k + 1) * 128, g0:g0 + TCB],
                         sem="LW%d" % hf)
                lwv = c.v("LW%d" % hf)
                wait("tensor", "LW%d" % hf, lwv)
                if hf == 1:
                    pass
                for m in range(ND):
                    for k in range(NDT):
                        em("tensor", lambda e, m=m, k=k, hf=hf, w_=w2h[0], md_=midt[k]: e.matmul(
                            pb[m][:, 0:TCB],
                            w_.rearrange("p (k m) -> p k m", m=1024)[:, k, m * 128:(m + 1) * 128],
                            md_[:], start=(hf == 0 and k == 0),
                            stop=(hf == 1 and k == NDT - 1)),
                            "MM" if k == NDT - 1 else None)
                mmv = c.v("MM")
                if hf == 0:
                    wait("sync", "MM", mmv)
            wait("vector", "MM", mmv)
            wait("vector", "LX0", lax)
            for m in range(ND):
                em("vector", lambda e, m=m, x2_=x2t[m], a_=a1t[m]: e.scalar_tensor_tensor(
                    x2_[:], pb[m][:, 0:TCB], sbp["fb2"][:, m:m + 1], a_[:],
                    ALU.add, ALU.add), "VV")
            wait("vector", "VV")
            for m in range(ND):
                em("vector", lambda e, m=m, x2_=x2t[m], x_=xt[m]: e.scalar_tensor_tensor(
                    x2_[:], x2_[:], modv[:, 16 + m:17 + m], x_[:],
                    ALU.mult, ALU.add), "VV")
            x2v = c.v("VV")
            if ti == 0:
                wait("vector", "VV", x2v)
                for m in range(ND):
                    em("vector", lambda e, m=m, x2_=x2t[m]: e.tensor_copy(
                        edge[:, m:m + 1], x2_[:, 0:1]), "VV")
            if ti == NCB - 1:
                wait("vector", "VV", x2v)
                for m in range(ND):
                    em("vector", lambda e, m=m, x2_=x2t[m]: e.tensor_copy(
                        edge[:, 8 + m:9 + m], x2_[:, TCB - 1:TCB]), "VV")
            wait("scalar", "VV")
            for m in range(ND):
                store(sp_x2[m * 128:(m + 1) * 128, g0:g0 + TCB], x2t[m][:], sem="SPa")

    w1t_bf = arA[:].bitcast(BF16)
    m127 = {}

    def p4a():
        c.barrier()
        CM1, CM2, CSW, CMD = 3200, 3202, 3204, 3206
        for L in (2, 1):
            vD, vE = View(arD), View(arE)
            wt = [vD.take(1024), vD.take(1024)]
            bt_f = vD.take(512)
            bt = bt_f.bitcast(BF16)
            rcol = [vD.take(1), vD.take(1)]
            acc = vD.take(1)
            swb = vD.take(1)
            i32t_ = [vE.take(1024).bitcast(mybir.dt.int32), vE.take(1024).bitcast(mybir.dt.int32)]
            cl_ = [arF[:, 0:1024], arF[:, 1024:2048]]
            bt2_f = vE.take(512)
            bt2 = bt2_f.bitcast(BF16)
            m127[L] = edge[:, 15 + L:16 + L]

            def tile_src(u, L=L):
                if L == 2:
                    return w_bn2.rearrange("(k p) m -> p k m", p=128)[:, u, :]
                k, q = u // 4, u % 4
                return w_bn1.rearrange("(k p) n -> p k n", p=128)[:, k, q * 1024:(q + 1) * 1024]

            em("vector", lambda e, a=acc: e.memset(a, 0.0), "VV")
            prog = []
            for u in range(NF):
                lw = "LWt%d" % (u % 2)
                if u >= 2:
                    wait("sync", "VV", prog[u - 2])
                load(wt[u % 2], tile_src(u), sem=lw)
                wait("vector", lw)
                em("vector", lambda e, w=wt[u % 2], r=rcol[u % 2]: e.tensor_reduce(
                    r, w, AX.X, ALU.add, apply_absolute_value=True), "VV")
                wait("vector", "VV")
                em("vector", lambda e, r=rcol[u % 2], a=acc: e.tensor_add(a, a, r), "VV")
                prog.append(c.v("VV"))
            wait("tensor", "VV")
            em("tensor", lambda e, a=acc: e.matmul(
                pb[6][0:1, 0:1], ones_col, a, start=True, stop=True), "MM")
            wait("vector", "MM")
            cm = stat[0:1, CM1 + (L - 1):CM1 + L]
            cw = stat[0:1, CSW:CSW + 1]
            cd = stat[0:1, CMD + (L - 1):CMD + L]
            em("vector", lambda e, cm=cm: e.tensor_copy(cm, pb[6][0:1, 0:1]), "VV")
            wait("vector", "VV")
            em("vector", lambda e, cm=cm: e.tensor_single_scalar(
                cm, cm, 1.0 / 4194304.0, ALU.mult), "VV")
            wait("vector", "VV")
            em("vector", lambda e, cm=cm: e.tensor_single_scalar(cm, cm, 1e-5, ALU.max), "VV")
            wait("vector", "VV")
            em("vector", lambda e, cm=cm, cw=cw: e.reciprocal(cw, cm), "VV")
            wait("vector", "VV")
            em("vector", lambda e, cm=cm, cd=cd: e.tensor_single_scalar(
                cd, cm, 1.0 / 127.0, ALU.mult), "VV")
            wait("scalar", "VV")
            bcast(swb, cw, 1)
            bcast(m127[L], cd, 1)
            wait("vector", "LB")
            prog = []
            for u in range(NF):
                lw = "LWt%d" % (u % 2)
                par = u % 2
                i32t = i32t_[par]
                cl = cl_[par]
                btu = (bt, bt2)[par]
                if u >= 2:
                    wait("sync", "VV", prog[u - 2][0])
                load(wt[par], tile_src(u), sem=lw)
                wait("vector", lw)
                if u >= 2:
                    wait("vector", "SC", prog[u - 2][1])
                em("vector", lambda e, w=wt[par], swb=swb, cl=cl: e.tensor_scalar(
                    cl, w, swb, None, ALU.mult), "VV")
                qv = c.v("VV")
                wait("scalar", "VV", qv)
                em("scalar", lambda e, i32t=i32t, cl=cl: e.copy(i32t, cl), "SC")
                i32v = c.v("SC")
                wait("scalar", "SC", i32v)
                em("scalar", lambda e, i32t=i32t, cl=cl: e.copy(cl, i32t), "SC")
                rbv = c.v("SC")
                wait("vector", "SC", rbv)
                em("vector", lambda e, cl=cl: e.tensor_single_scalar(cl, cl, 1.0, ALU.min), "VV")
                wait("vector", "VV")
                em("vector", lambda e, cl=cl: e.tensor_single_scalar(cl, cl, -1.0, ALU.max), "VV")
                wait("vector", "VV")
                if L == 1:
                    em("vector", lambda e, u=u, cl=cl: e.tensor_copy(
                        w1t_bf[:, u * 1024:(u + 1) * 1024], cl), "VV")
                    cpv = c.v("VV")
                else:
                    if u >= 2:
                        wait("vector", "SPw", prog[u - 2][2])
                    em("vector", lambda e, b=btu, cl=cl: e.tensor_copy(b, cl), "VV")
                    cpv = c.v("VV")
                    wait("scalar", "VV", cpv)
                    store(sp_w2t[:, u * 1024:(u + 1) * 1024], btu, sem="SPw")
                prog.append((cpv, i32v, c.v("SPw") if L == 2 else 0))
            c.barrier()
        if DEBUG:
            for j in range(8):
                store(dbg_w1t[:, j * 4096:(j + 1) * 4096],
                      w1t_bf[:, j * 4096:(j + 1) * 4096], sem="SPw")
            c.barrier()

    bn_tails = arD[:, 3900:3964]

    def p4c():
        x2src = sp_x2 if x2_in is None else x2_in
        wait("vector", "VV")
        em("vector", lambda e: e.memset(bn_tails[:], 0.0), "VV")
        wait("vector", "VV")
        em("vector", lambda e: e.tensor_copy(
            bn_tails.rearrange("p (f two) -> p f two", two=2)[:, :, 1], mhL[:]), "VV")
        for ci in range(NCB):
            g0 = ci * TCB
            c.barrier()
            vB, vC, vD, vE = View(arB), View(arC), View(arD), View(arE)
            x2t = [vB.take(TCB) for _ in range(ND)]
            hbt = [vB.take(TCB) for _ in range(ND)]
            qx_f = vC.take(2048)
            qx = qx_f.bitcast(BF16)
            sq = [vC.take(TCB), vC.take(TCB)]
            rep_a, rep_b = vC.take(TCB), vC.take(TCB)
            rmax = vC.take(TCB)
            tmp = vC.take(TCB)
            r1 = vC.take(TCB)
            sx1 = vC.take(TCB)
            ds1 = vC.take(TCB)
            mid1 = [vC.take(TCB + 2), vC.take(TCB + 2)]
            oc = [vD.take(TCB), vD.take(TCB)]
            ms = [vD.take(TCB), vD.take(TCB)]
            rmax2w = vD.take(TCB)
            sqr = [vD.take(TCB), vD.take(TCB)]
            acl4 = vD.take(4)
            i32t = [vE.take(TCB).bitcast(mybir.dt.int32), vE.take(TCB).bitcast(mybir.dt.int32)]
            for i in range(ND):
                load(x2t[i], x2src[i * 128:(i + 1) * 128, g0:g0 + TCB], sem="LX0")
            lxv = c.v("LX0")
            for eng in ("tensor", "scalar", "vector"):
                wait(eng, "LX0", lxv)
            em("gpsimd", lambda e, r=rmax2w: e.memset(r, 0.0), "GP")
            gp_init = c.v("GP")
            emit_ln(x2t, hbt, TCB,
                    [modv[:, 32 + i:33 + i] for i in range(ND)],
                    [modv[:, 24 + i:25 + i] for i in range(ND)],
                    pb[7], sq, rep_a, rep_b)
            wait("vector", "VV")
            em("vector", lambda e: e.scalar_tensor_tensor(
                rmax[:], hbt[0][:], consts[:, 1:2], hbt[0][:], ALU.mult, ALU.max), "VV")
            for i in range(1, ND):
                wait("vector", "VV")
                em("vector", lambda e, i=i: e.scalar_tensor_tensor(
                    tmp[:], hbt[i][:], consts[:, 1:2], hbt[i][:], ALU.mult, ALU.max), "VV")
                wait("vector", "VV")
                em("vector", lambda e: e.tensor_max(rmax[:], rmax[:], tmp[:]), "VV")
            wait("vector", "VV")
            em("vector", lambda e: e.tensor_copy(tmp[0:64, :], rmax[64:128, :]), "VV")
            wait("vector", "VV")
            em("vector", lambda e: e.tensor_max(rmax[0:64, :], rmax[0:64, :], tmp[0:64, :]), "VV")
            wait("vector", "VV")
            em("vector", lambda e: e.tensor_copy(tmp[0:32, :], rmax[32:64, :]), "VV")
            wait("vector", "VV")
            em("vector", lambda e: e.tensor_max(rmax[0:32, :], rmax[0:32, :], tmp[0:32, :]), "VV")
            vfold = c.v("VV")
            wait("tensor", "VV", vfold)
            for blk in range(4):
                em("tensor", lambda e, blk=blk: e.transpose(
                    pb[6][:, blk * 32:(blk + 1) * 32],
                    rmax[0:32, blk * 128:(blk + 1) * 128], ident[0:32, 0:32]), "MM")
            wait("vector", "MM")
            for blk in range(4):
                em("vector", lambda e, blk=blk: e.tensor_reduce(
                    acl4[:, blk:blk + 1], pb[6][:, blk * 32:(blk + 1) * 32], AX.X, ALU.max),
                    "VV" if blk == 3 else None)
            wait("scalar", "VV")
            ri = _ring[0] % 16
            _ring[0] += 1
            for blk in range(4):
                store(sp_row[ri:ri + 1, blk * 128:(blk + 1) * 128], acl4[:, blk:blk + 1], sem="SR")
            wait("sync", "SR")
            c.dsem("LB")
            c.em("sync", lambda e, d=r1, ri=ri: e.dma_start(
                out=d, in_=sp_row[ri:ri + 1, 0:TCB].partition_broadcast(128).squeeze(1)), "LB", 16)
            wait("vector", "LB")
            em("vector", lambda e: e.tensor_single_scalar(r1[:], r1[:], 1e-5, ALU.max), "VV")
            wait("vector", "VV")
            em("vector", lambda e: e.reciprocal(sx1[:], r1[:]), "VV")
            wait("vector", "VV")
            em("vector", lambda e: e.tensor_single_scalar(sx1[:], sx1[:], 127.0, ALU.mult), "VV")
            wait("vector", "VV")
            em("vector", lambda e: e.tensor_scalar(ds1[:], r1[:], m127[1], None, ALU.mult), "VV")
            qprog = []
            for k in range(ND):
                par = k % 2
                if k >= 2:
                    wait("vector", "SC", qprog[k - 2])
                wait("vector", "VV")
                em("vector", lambda e, k=k, par=par: e.tensor_mul(sq[par][:], hbt[k][:], sx1[:]), "VV")
                wait("scalar", "VV")
                em("scalar", lambda e, par=par: e.copy(i32t[par], sq[par][:]), "SC")
                wait("scalar", "SC")
                em("scalar", lambda e, k=k, par=par: e.copy(
                    qx[:, k * TCB:(k + 1) * TCB], i32t[par]), "SC")
                qprog.append(c.v("SC"))
            qxv = c.v("SC")
            wait("tensor", "SC", qxv)
            wait("gpsimd", "GP", gp_init)
            evv, scv, gpv, mxv = {}, {}, {}, {}
            for f in range(NF):
                par = f % 2
                bank = pb[par]
                if f >= 2:
                    wait("tensor", "VV", evv[f - 2])
                for k in range(ND):
                    em("tensor", lambda e, f=f, k=k, bank=bank: e.matmul(
                        bank[:, 0:TCB],
                        w1t_bf[:, k * 4096 + f * 128:k * 4096 + f * 128 + 128],
                        qx[:, k * TCB:(k + 1) * TCB],
                        start=(k == 0), stop=(k == ND - 1)),
                        "MM" if k == ND - 1 else None)
                mmv = c.v("MM")
                wait("vector", "MM", mmv)
                if f >= 2:
                    wait("vector", "SC", scv[f - 2])
                em("vector", lambda e, par=par, bank=bank, m=mid1[par]: e.tensor_tensor(
                    m[:, 2:TCB + 2], bank[:, 0:TCB], ds1[:], ALU.mult))
                em("vector", lambda e, par=par, f=f, m=mid1[par]: e.tensor_copy(
                    m[:, 0:2], bn_tails[:, f * 2:f * 2 + 2]), "VV")
                evv[f] = c.v("VV")
                wait("scalar", "VV", evv[f])
                if f >= 2:
                    wait("scalar", "GP", gpv[f - 2])
                em("scalar", lambda e, par=par, f=f, m=mid1[par], o=oc[par]: e.activation(
                    o[:], m[:, 0:TCB], AF.Copy, scale=sbp["bconv_w"][:, f * 3:f * 3 + 1]), "SC")
                t0v = c.v("SC")
                wait("vector", "SC", t0v)
                em("vector", lambda e, par=par, f=f, m=mid1[par], o=oc[par]: e.scalar_tensor_tensor(
                    o[:], m[:, 1:TCB + 1], sbp["bconv_w"][:, f * 3 + 1:f * 3 + 2], o[:],
                    ALU.mult, ALU.add))
                em("vector", lambda e, par=par, f=f, m=mid1[par], o=oc[par]: e.scalar_tensor_tensor(
                    o[:], m[:, 2:TCB + 2], sbp["bconv_w"][:, f * 3 + 2:f * 3 + 3], o[:],
                    ALU.mult, ALU.add))
                em("vector", lambda e, par=par, f=f, m=mid1[par]: e.tensor_copy(
                    bn_tails[:, f * 2:f * 2 + 2], m[:, TCB:TCB + 2]), "VV")
                cvv = c.v("VV")
                wait("scalar", "VV", cvv)
                em("scalar", lambda e, par=par, f=f, o=oc[par], s=ms[par]: e.activation(
                    s[:], o[:], AF.Silu, bias=sbp["bconv_b"][:, f:f + 1]), "SC")
                scv[f] = c.v("SC")
                wait("scalar", "SC", scv[f])
                if ci == 0:
                    store(sp_bmid2[f * 128:(f + 1) * 128, 0:TCB - 1],
                          ms[par][:, 1:TCB], sem="SPm")
                else:
                    store(sp_bmid2[f * 128:(f + 1) * 128, g0 - 1:g0 + TCB - 1],
                          ms[par][:], sem="SPm")
                wait("gpsimd", "SC", scv[f])
                if f >= 2:
                    wait("gpsimd", "VV", mxv[f - 2])
                em("gpsimd", lambda e, par=par: e.tensor_tensor(
                    sqr[par][:], ms[par][:], ms[par][:], ALU.mult), "GP")
                gpv[f] = c.v("GP")
                wait("vector", "GP", gpv[f])
                em("vector", lambda e, par=par: e.tensor_max(
                    rmax2w[:], rmax2w[:], sqr[par][:]), "VV")
                mxv[f] = c.v("VV")
            wait("vector", "VV", mxv[NF - 1])
            em("vector", lambda e: e.tensor_copy(tmp[0:64, :], rmax2w[64:128, :]), "VV")
            wait("vector", "VV")
            em("vector", lambda e: e.tensor_max(rmax2w[0:64, :], rmax2w[0:64, :], tmp[0:64, :]), "VV")
            wait("vector", "VV")
            em("vector", lambda e: e.tensor_copy(tmp[0:32, :], rmax2w[32:64, :]), "VV")
            wait("vector", "VV")
            em("vector", lambda e: e.tensor_max(rmax2w[0:32, :], rmax2w[0:32, :], tmp[0:32, :]), "VV")
            vfold2 = c.v("VV")
            wait("tensor", "VV", vfold2)
            for blk in range(4):
                em("tensor", lambda e, blk=blk: e.transpose(
                    pb[6][:, blk * 32:(blk + 1) * 32],
                    rmax2w[0:32, blk * 128:(blk + 1) * 128], ident[0:32, 0:32]), "MM")
            wait("vector", "MM")
            for blk in range(4):
                em("vector", lambda e, blk=blk: e.tensor_reduce(
                    acl4[:, blk:blk + 1], pb[6][:, blk * 32:(blk + 1) * 32], AX.X, ALU.max),
                    "VV" if blk == 3 else None)
            wait("scalar", "VV")
            for blk in range(4):
                if ci == 0 and blk == 0:
                    store(sp_rm2[0:1, 0:127], acl4[1:128, 0:1], sem="SR")
                else:
                    store(sp_rm2[0:1, g0 - 1 + blk * 128:g0 - 1 + (blk + 1) * 128],
                          acl4[:, blk:blk + 1], sem="SR")

    def p4d():
        c.barrier()
        vE = View(arE)
        oc32 = vE.take(32)
        tm32 = vE.take(32)
        es32 = vE.take(32)
        mcol = vE.take(1)
        fcol = vE.take(1)
        cell = vE.take(2)
        w3 = sbp["bconv_w"].rearrange("p (f th) -> p f th", th=3)
        t2 = bn_tails.rearrange("p (f two) -> p f two", two=2)
        wait("vector", "VV")
        em("vector", lambda e: e.tensor_tensor(oc32[:], t2[:, :, 0], w3[:, :, 0], ALU.mult), "VV")
        wait("vector", "VV")
        em("vector", lambda e: e.tensor_tensor(tm32[:], t2[:, :, 1], w3[:, :, 1], ALU.mult), "VV")
        wait("vector", "VV")
        em("vector", lambda e: e.tensor_add(oc32[:], oc32[:], tm32[:]), "VV")
        wait("vector", "VV")
        em("vector", lambda e: e.tensor_tensor(tm32[:], mhR[:], w3[:, :, 2], ALU.mult), "VV")
        wait("vector", "VV")
        em("vector", lambda e: e.tensor_add(oc32[:], oc32[:], tm32[:]), "VV")
        wait("vector", "VV")
        em("vector", lambda e: e.tensor_add(oc32[:], oc32[:], sbp["bconv_b"][:, 0:32]), "VV")
        wait("scalar", "VV")
        em("scalar", lambda e: e.activation(es32[:], oc32[:], AF.Silu), "SC")
        wait("scalar", "SC")
        c.em("scalar", lambda e: e.dma_start(
            out=sp_bmid2.rearrange("(f p) t -> p f t", p=128)[:, :, T - 1:T],
            in_=es32[:].unsqueeze(2), allow_slow_non_contiguous=True), c.dsem("SPm"), 16)
        wait("vector", "SC")
        em("vector", lambda e: e.tensor_reduce(mcol, es32[:], AX.X, ALU.max,
                                               apply_absolute_value=True), "VV")
        wait("vector", "VV")
        em("vector", lambda e: e.tensor_copy(fcol[0:64, :], mcol[64:128, :]), "VV")
        wait("vector", "VV")
        em("vector", lambda e: e.tensor_max(mcol[0:64, :], mcol[0:64, :], fcol[0:64, :]), "VV")
        wait("vector", "VV")
        em("vector", lambda e: e.tensor_copy(fcol[0:32, :], mcol[32:64, :]), "VV")
        wait("vector", "VV")
        em("vector", lambda e: e.tensor_max(mcol[0:32, :], mcol[0:32, :], fcol[0:32, :]), "VV")
        mf = c.v("VV")
        wait("tensor", "VV", mf)
        em("tensor", lambda e: e.transpose(pb[6][0:1, 0:32], mcol[0:32, 0:1], ident[0:32, 0:32]), "MM")
        wait("vector", "MM")
        em("vector", lambda e: e.tensor_reduce(cell[0:1, 0:1], pb[6][0:1, 0:32], AX.X, ALU.max), "VV")
        wait("vector", "VV")
        em("vector", lambda e: e.tensor_mul(cell[0:1, 1:2], cell[0:1, 0:1], cell[0:1, 0:1]), "VV")
        wait("scalar", "VV")
        store(sp_rm2[0:1, T - 1:T], cell[0:1, 1:2], sem="SR")
        c.barrier()
        w2t_res = arA[:].bitcast(BF16)
        load(w2t_res[:, 0:NF * 1024], sp_w2t[:], sem="LW0")
        x2src = sp_x2 if x2_in is None else x2_in
        for ci in range(NCB):
            g0 = ci * TCB
            c.barrier()
            vB, vC, vD, vE = View(arB), View(arC), View(arD), View(arE)
            mt = [ (vB if k < 16 else vC).take(TCB) for k in range(NF) ]
            r2 = vD.take(TCB)
            sx2 = vD.take(TCB)
            ds2 = vD.take(TCB)
            qt = [vD.take(TCB), vD.take(TCB)]
            yt = vD.take(TCB)
            i32t = [vE.take(TCB).bitcast(mybir.dt.int32), vE.take(TCB).bitcast(mybir.dt.int32)]
            x2p = [vE.take(TCB), vE.take(TCB)]
            for k in range(NF):
                load(mt[k], sp_bmid2[k * 128:(k + 1) * 128, g0:g0 + TCB], sem="LX0")
            lxv = c.v("LX0")
            c.dsem("LB")
            c.em("sync", lambda e, d=r2, g0=g0: e.dma_start(
                out=d, in_=sp_rm2[0:1, g0:g0 + TCB].partition_broadcast(128).squeeze(1)), "LB", 16)
            wait("scalar", "LB")
            em("scalar", lambda e: e.activation(r2[:], r2[:], AF.Sqrt), "SC")
            wait("vector", "SC")
            em("vector", lambda e: e.tensor_single_scalar(r2[:], r2[:], 1e-5, ALU.max), "VV")
            wait("vector", "VV")
            em("vector", lambda e: e.reciprocal(sx2[:], r2[:]), "VV")
            wait("vector", "VV")
            em("vector", lambda e: e.tensor_single_scalar(sx2[:], sx2[:], 127.0, ALU.mult), "VV")
            wait("vector", "VV")
            em("vector", lambda e: e.tensor_scalar(ds2[:], r2[:], m127[2], None, ALU.mult), "VV")
            wait("vector", "LX0", lxv)
            wait("tensor", "LW0")
            qv = {}
            for k in range(NF):
                par = k % 2
                if k >= 2:
                    wait("vector", "SC", qv[k - 2])
                wait("vector", "VV")
                em("vector", lambda e, k=k, par=par: e.tensor_mul(qt[par][:], mt[k][:], sx2[:]), "VV")
                wait("scalar", "VV")
                em("scalar", lambda e, par=par: e.copy(i32t[par], qt[par][:]), "SC")
                wait("scalar", "SC")
                arena = arB if k < 16 else arC
                boff = (mt[k].offset_cols if False else None)
                em("scalar", lambda e, k=k, par=par, arena=arena: e.copy(
                    arena[:].bitcast(BF16)[:, (k % 16) * 1024:(k % 16) * 1024 + TCB],
                    i32t[par]), "SC")
                qv[k] = c.v("SC")
                wait("tensor", "SC", qv[k])
                for m in range(ND):
                    em("tensor", lambda e, k=k, m=m, arena=arena: e.matmul(
                        pb[m][:, 0:TCB],
                        w2t_res[:, k * 1024 + m * 128:k * 1024 + m * 128 + 128],
                        arena[:].bitcast(BF16)[:, (k % 16) * 1024:(k % 16) * 1024 + TCB],
                        start=(k == 0), stop=(k == NF - 1)),
                        "MM" if (k == NF - 1) else None)
            mmv = c.v("MM")
            wait("vector", "MM", mmv)
            evs = []
            for m in range(ND):
                lxe = "LXe%d" % (m % 2)
                c.dsem(lxe)
                if m >= 2:
                    wait("sync", "VV", evs[m - 2])
                load(x2p[m % 2], x2src[m * 128:(m + 1) * 128, g0:g0 + TCB], sem=lxe)
                lev = c.v(lxe)
                em("vector", lambda e, m=m: e.tensor_tensor(
                    yt[:], pb[m][:, 0:TCB], ds2[:], ALU.mult), "VV")
                wait("vector", lxe, lev)
                wait("vector", "VV")
                em("vector", lambda e, m=m, par=m % 2: e.scalar_tensor_tensor(
                    yt[:], yt[:], modv[:, 40 + m:41 + m], x2p[par][:],
                    ALU.mult, ALU.add), "VV")
                evs.append(c.v("VV"))
                wait("scalar", "VV")
                store(out_ext[m * 128:(m + 1) * 128, g0:g0 + TCB], yt[:], sem="SO")
                wait("vector", "SO")
        c.barrier()

    mhL = arD[:, 4096:4128]
    mhR = arD[:, 4128:4160]

    def p4b():
        CMD = 3206
        c.barrier()
        if edge_in is not None:
            load(edge[:, 0:16], edge_in, sem="LE")
            wait("gpsimd", "LE")
        c.dsem("GD")
        em("gpsimd", lambda e: e.dma_start(
            out=ag2_in.rearrange("o (p n) -> (o p) n", p=128), in_=edge[:, 0:16]), "GD", 16)
        wait("gpsimd", "GD")
        groups = [[0, 1], [2, 3], [4, 5], [6, 7]]
        em("gpsimd", lambda e, groups=groups: e.collective_compute(
            "AllGather", ALU.bypass, replica_groups=groups,
            ins=[ag2_in.opt()], outs=[ag2_out.opt()]), "CC")
        wait("sync", "CC")
        vE = View(arE)
        xv = [vE.take(8), vE.take(8)]
        sq8 = vE.take(8)
        hb8 = [vE.take(8), vE.take(8)]
        mcol = vE.take(1)
        fcol = vE.take(1)
        rowc = vE.take(4)
        bc = [vE.take(1) for _ in range(6)]
        qlr_f = vE.take(8)
        qlr = qlr_f.bitcast(BF16)
        i8 = vE.take(16).bitcast(mybir.dt.int32)
        c.dsem("LH2")
        c.em("sync", lambda e: e.dma_start(
            out=xv[0], in_=ag2_out[0:1, :].rearrange("o (p n) -> (o p) n", p=128)[:, 8:16]), "LH2", 16)
        c.em("sync", lambda e: e.dma_start(
            out=xv[1], in_=ag2_out[1:2, :].rearrange("o (p n) -> (o p) n", p=128)[:, 0:8]), "LH2", 16)
        lh = c.v("LH2")
        for eng in ("tensor", "scalar", "vector"):
            wait(eng, "LH2", lh)
        for v in range(2):
            x8 = xv[v]
            em("tensor", lambda e, x8=x8: e.matmul(pb[6][0:1, 0:8], ones_col, x8,
                                                   start=True, stop=True), "MM")
            wait("scalar", "MM")
            em("scalar", lambda e, x8=x8: e.activation(sq8[:], x8, AF.Square), "SC")
            wait("tensor", "SC")
            wait("vector", "MM")
            em("vector", lambda e: e.tensor_reduce(rowc[0:1, 0:1], pb[6][0:1, 0:8], AX.X, ALU.add), "VV")
            wait("tensor", "VV")
            em("tensor", lambda e: e.matmul(pb[6][0:1, 8:16], ones_col, sq8[:],
                                            start=True, stop=True), "MM")
            wait("vector", "MM")
            em("vector", lambda e: e.tensor_reduce(rowc[0:1, 1:2], pb[6][0:1, 8:16], AX.X, ALU.add), "VV")
            wait("vector", "VV")
            em("vector", lambda e: e.tensor_single_scalar(rowc[0:1, 0:2], rowc[0:1, 0:2], 1.0 / D, ALU.mult), "VV")
            wait("vector", "VV")
            em("vector", lambda e: e.tensor_mul(rowc[0:1, 2:3], rowc[0:1, 0:1], rowc[0:1, 0:1]), "VV")
            wait("vector", "VV")
            em("vector", lambda e: e.tensor_sub(rowc[0:1, 1:2], rowc[0:1, 1:2], rowc[0:1, 2:3]), "VV")
            wait("scalar", "VV")
            em("scalar", lambda e: e.activation(rowc[0:1, 1:2], rowc[0:1, 1:2], AF.Sqrt, bias=1e-5), "SC")
            wait("vector", "SC")
            em("vector", lambda e: e.reciprocal(rowc[0:1, 1:2], rowc[0:1, 1:2]), "VV")
            wait("scalar", "VV")
            bcast(bc[0], rowc[0:1, 0:1], 1)
            bcast(bc[1], rowc[0:1, 1:2], 1)
            wait("vector", "LB")
            hb = hb8[v]
            em("vector", lambda e, x8=x8, hb=hb: e.tensor_scalar(hb, x8, bc[0], None, ALU.subtract), "VV")
            wait("vector", "VV")
            em("vector", lambda e, hb=hb: e.tensor_scalar(hb, hb, bc[1], None, ALU.mult), "VV")
            wait("vector", "VV")
            em("vector", lambda e, hb=hb: e.tensor_mul(hb, hb, modv[:, 32:40]), "VV")
            wait("vector", "VV")
            em("vector", lambda e, hb=hb: e.tensor_add(hb, hb, modv[:, 24:32]), "VV")
            wait("vector", "VV")
            if DEBUG:
                wait("scalar", "VV")
                store(sp_hh[:, 64 + v * 8:64 + (v + 1) * 8], hb, sem="SPh")
                wait("vector", "SPh")
            em("vector", lambda e, hb=hb: e.tensor_reduce(mcol, hb, AX.X, ALU.max,
                                                          apply_absolute_value=True), "VV")
            wait("vector", "VV")
            em("vector", lambda e: e.tensor_copy(fcol[0:64, :], mcol[64:128, :]), "VV")
            wait("vector", "VV")
            em("vector", lambda e: e.tensor_tensor(mcol[0:64, :], mcol[0:64, :], fcol[0:64, :], ALU.max), "VV")
            wait("vector", "VV")
            em("vector", lambda e: e.tensor_copy(fcol[0:32, :], mcol[32:64, :]), "VV")
            wait("vector", "VV")
            em("vector", lambda e: e.tensor_tensor(mcol[0:32, :], mcol[0:32, :], fcol[0:32, :], ALU.max), "VV")
            mfold = c.v("VV")
            wait("tensor", "VV", mfold)
            em("tensor", lambda e: e.transpose(pb[6][0:1, 16:48], mcol[0:32, 0:1], ident[0:32, 0:32]), "MM")
            wait("vector", "MM")
            em("vector", lambda e: e.tensor_reduce(rowc[0:1, 2:3], pb[6][0:1, 16:48], AX.X, ALU.max), "VV")
            wait("vector", "VV")
            em("vector", lambda e: e.tensor_single_scalar(rowc[0:1, 2:3], rowc[0:1, 2:3], 1e-5, ALU.max), "VV")
            wait("vector", "VV")
            em("vector", lambda e: e.tensor_mul(rowc[0:1, 3:4], rowc[0:1, 2:3], stat[0:1, CMD:CMD + 1]), "VV")
            wait("vector", "VV")
            em("vector", lambda e: e.reciprocal(rowc[0:1, 2:3], rowc[0:1, 2:3]), "VV")
            wait("vector", "VV")
            em("vector", lambda e: e.tensor_single_scalar(rowc[0:1, 2:3], rowc[0:1, 2:3], 127.0, ALU.mult), "VV")
            wait("scalar", "VV")
            bcast(bc[2 + v], rowc[0:1, 2:3], 1)
            bcast(bc[4 + v], rowc[0:1, 3:4], 1)
            wait("vector", "LB")
            em("vector", lambda e, hb=hb, v=v: e.tensor_scalar(hb, hb, bc[2 + v], None, ALU.mult), "VV")
            wait("scalar", "VV")
            em("scalar", lambda e, v=v: e.copy(i8[:, v * 8:(v + 1) * 8], hb8[v]), "SC")
            wait("scalar", "SC")
            em("scalar", lambda e, v=v: e.copy(
                qlr.rearrange("p (k two) -> p k two", two=2)[:, :, v], i8[:, v * 8:(v + 1) * 8]), "SC")
            if DEBUG:
                wait("scalar", "SC")
                store(sp_stat[0:1, 3000 + v * 4:3004 + v * 4], rowc[0:1, 0:4], sem="SPh")
                store(sp_hh[:, 96 + v * 8:96 + (v + 1) * 8], hb, sem="SPh")
        qv_ = c.v("SC")
        if DEBUG:
            wait("vector", "SC", qv_)
            em("vector", lambda e: e.tensor_copy(sq8[:], i8[:, 0:8]), "VV")
            em("vector", lambda e: e.tensor_copy(hb8[0][:], i8[:, 8:16]), "VV")
            wait("scalar", "VV")
            store(sp_hh[:, 128:136], sq8[:], sem="SPh")
            store(sp_hh[:, 136:144], hb8[0][:], sem="SPh")
            wait("vector", "SPh")
            wait("tensor", "SPh")
        wait("tensor", "SC", qv_)
        evg = []
        for f in range(NF):
            bank = pb[4 + (f % 2)]
            if f >= 2:
                wait("tensor", evg[f - 2][0], evg[f - 2][1])
            for k in range(ND):
                em("tensor", lambda e, f=f, k=k, bank=bank: e.matmul(
                    bank[:, 0:2],
                    w1t_bf[:, (k * 4096 + f * 128) * 1:(k * 4096 + f * 128) + 128],
                    qlr[:, k * 2:(k + 1) * 2], start=(k == 0), stop=(k == ND - 1)),
                    "MM" if k == ND - 1 else None)
            wait("vector", "MM")
            em("vector", lambda e, f=f, bank=bank: e.tensor_scalar(
                mhL[:, f:f + 1], bank[:, 0:1], bc[4], None, ALU.mult))
            em("vector", lambda e, f=f, bank=bank: e.tensor_scalar(
                mhR[:, f:f + 1], bank[:, 1:2], bc[5], None, ALU.mult), "VV")
            evg.append(("VV", c.v("VV")))
        wait("vector", "VV")
        em("vector", lambda e: e.tensor_scalar(mhL[:], mhL[:], consts[:, 3:4], None, ALU.mult), "VV")
        wait("vector", "VV")
        em("vector", lambda e: e.tensor_scalar(mhR[:], mhR[:], consts[:, 5:6], None, ALU.mult), "VV")
        if DEBUG:
            c.barrier()
            store(sp_cl[:, 0:32], mhL[:])
            store(sp_cl[:, 32:64], mhR[:])
            c.barrier()

    if "P0" in PHASES:
        p0()
    if "A12" in PHASES:
        a12()
    if "A4" in PHASES:
        a4()
    if "B1" in PHASES:
        b1()
    if "B2" in PHASES:
        b2a()
        b2b()
        b2c()
    if "P4A" in PHASES:
        p4a()
    if "P4B" in PHASES:
        p4b()
    if "P4C" in PHASES:
        p4c()
    if "P4D" in PHASES:
        p4d()
    else:
        c.dsem("GD")
        em("gpsimd", lambda e: e.dma_start(out=out_ext[0:128, 0:8], in_=modv[:, 0:8]), "GD", 16)
        wait("gpsimd", "GD")

    c.barrier()

    with nc.Block() as block:
        @block.sync
        def _(eng):
            for f in c.ops["sync"]:
                f(eng)

        @block.vector
        def _(eng):
            for f in c.ops["vector"]:
                f(eng)

        @block.scalar
        def _(eng):
            for f in c.ops["scalar"]:
                f(eng)

        @block.tensor
        def _(eng):
            for f in c.ops["tensor"]:
                f(eng)

        @block.gpsimd
        def _(eng):
            for f in c.ops["gpsimd"]:
                f(eng)

    n_inst = sum(len(v) for v in c.ops.values())
    print("planned ops:", n_inst)
    return nc


def make_inmaps(inp, cores=range(8)):
    def fm(a):
        return np.ascontiguousarray(np.asarray(a, np.float32).T)

    def dmaj(v, nt):
        return np.ascontiguousarray(np.asarray(v, np.float32).reshape(nt, 128).T)

    maps = []
    for cid in cores:
        b, h = cid // 2, cid % 2
        t0 = h * T
        x = np.asarray(inp["L0_attn_x"][b], np.float32)
        m = {
            "x": fm(x[t0:t0 + T]),
            "xhalo": fm(x[t0 - 3:t0]) if h == 1 else np.zeros((D, 3), np.float32),
            "w_inproj": np.asarray(inp["L0_attn_in_proj_w"], np.float32),
            "w_xproj": np.asarray(inp["L0_attn_x_proj_w"], np.float32),
            "w_dtproj": np.asarray(inp["L0_attn_dt_proj_w"], np.float32),
            "w_outproj": np.asarray(inp["L0_attn_out_proj_w"], np.float32),
            "w_ffn1": np.asarray(inp["L0_attn_ffn_w1"], np.float32),
            "w_ffn2": np.asarray(inp["L0_attn_ffn_w2"], np.float32),
            "w_bn1": np.ascontiguousarray(np.asarray(inp["L0_ffn_w1"], np.float32).T),
            "w_bn2": np.ascontiguousarray(np.asarray(inp["L0_ffn_w2"], np.float32).T),
            "w_mod": np.asarray(inp["L0_wmod"], np.float32),
            "c_vec": dmaj(inp["L0_c"][b], ND),
            "ln1g": dmaj(inp["L0_attn_ln1_g"], ND),
            "ln1b": dmaj(inp["L0_attn_ln1_b"], ND),
            "ln2g": dmaj(inp["L0_attn_ln2_g"], ND),
            "ln2b": dmaj(inp["L0_attn_ln2_b"], ND),
            "mconv_w": np.ascontiguousarray(
                np.asarray(inp["L0_attn_conv_w"], np.float32).reshape(NDT, 128, 4).transpose(1, 0, 2).reshape(128, NDT * 4)),
            "mconv_b": dmaj(inp["L0_attn_conv_b"], NDT),
            "dt_bias": dmaj(inp["L0_attn_dt_proj_b"], NDT),
            "a_log": np.ascontiguousarray(
                np.asarray(inp["L0_attn_A_log"], np.float32).reshape(NDT, 128, DS).transpose(1, 0, 2).reshape(128, NDT * DS)),
            "d_vec": dmaj(inp["L0_attn_D"], NDT),
            "bconv_w": np.ascontiguousarray(
                np.asarray(inp["L0_ffn_conv_w"], np.float32).reshape(NF, 128, 3).transpose(1, 0, 2).reshape(128, NF * 3)),
            "bconv_b": dmaj(inp["L0_ffn_conv_b"], NF),
            "fb1": dmaj(inp["L0_attn_ffn_b1"], NF),
            "fb2": dmaj(inp["L0_attn_ffn_b2"], ND),
            "hmask": np.full((128, 3), 1.0 if h == 1 else 0.0, np.float32),
            "consts": np.tile(np.array([[1.0, -1.0, 1e-5, float(h),
                                         float(h), 1.0 - float(h), 0.0, 0.0]], np.float32), (128, 1)),
            "consts2": np.concatenate([np.full((128, 8), 1.0, np.float32),
                                       np.full((128, 8), 0.25, np.float32)], axis=1),
            "ident_in": np.eye(128, dtype=np.float32),
        }
        maps.append(m)
    return maps




_NC_CACHE = [None]


def kernel(**inputs):
    if _NC_CACHE[0] is None:
        _NC_CACHE[0] = build_nc()
    nc = _NC_CACHE[0]
    maps = make_inmaps(inputs, cores=range(8))
    res = run_bass_kernel_spmd(nc, maps, core_ids=list(range(8)))
    out = np.empty((4, 8192, 1024), np.float32)
    for cid in range(8):
        b, h = cid // 2, cid % 2
        out[b, h * T:(h + 1) * T, :] = np.asarray(res.results[cid]["out"]).T
    return out


# version 10
# speedup vs baseline: 1.0420x; 1.0420x over previous
import sys
sys.path.insert(0, '/opt/trn_rl_repo')
import numpy as np
import concourse.bass as bass
import concourse.mybir as mybir
from concourse.bass_utils import run_bass_kernel_spmd

F32 = mybir.dt.float32
BF16 = mybir.dt.bfloat16
AF = mybir.ActivationFunctionType
ALU = mybir.AluOpType
AX = mybir.AxisListType

D, DI, DS, DTR, DFF, T = 1024, 2048, 16, 64, 4096, 4096
ND, NDT, NF = 8, 16, 32
TC1, TC4, TCB = 1024, 256, 512
NC1, NC4, NCB = T // TC1, T // TC4, T // TCB

DEBUG = False
DEV_NCHUNKS = 4
DEV_STAGE = DEV_SUB = DEV_C2SUB = 9
DEV_NOEVAC = DEV_NOMM = 0
PHASES = ("P0", "A12", "A4", "B1", "B2", "P4A", "P4B", "P4C", "P4D")


class Ctx:
    COMPUTE = ("VV", "SC", "GP", "MM", "CC")

    def __init__(self, nc):
        self.nc = nc
        self.ops = {k: [] for k in ("sync", "vector", "scalar", "tensor", "gpsimd")}
        self.sem = {}
        for name in self.COMPUTE:
            self.sem[name] = [nc.alloc_semaphore(name), 0]
        self.dma_sems = []

    def dsem(self, name):
        if name not in self.sem:
            self.sem[name] = [self.nc.alloc_semaphore(name), 0]
            self.dma_sems.append(name)
        return name

    def em(self, eng, fn, inc=None, n=1):
        if inc is None:
            self.ops[eng].append(fn)
        else:
            s = self.sem[inc]
            s[1] += n
            self.ops[eng].append(lambda e, f=fn, h=s[0], k=n: f(e).then_inc(h, k))

    def wait(self, eng, sem, v=None):
        s = self.sem[sem]
        t = s[1] if v is None else v
        if t > 0:
            self.ops[eng].append(lambda e, h=s[0], tv=t: e.wait_ge(h, tv))

    def v(self, sem):
        return self.sem[sem][1]

    def barrier(self):
        self.wait("tensor", "VV")
        self.em("vector", self.vec_inc_op, "VV")
        self.em("scalar", self.sc_inc_op, "SC")
        self.em("tensor", self.mm_inc_op, "MM")
        for sem in ("VV", "SC", "MM"):
            self.wait("gpsimd", sem)
        for sem in self.dma_sems:
            self.wait("gpsimd", sem)
        self.em("gpsimd", lambda e: e.nop(), "GP")
        gp = self.v("GP")
        for eng in ("sync", "vector", "scalar", "tensor"):
            self.wait(eng, "GP", gp)

    def load(self, dst, src, sem="L0"):
        self.dsem(sem)
        self.em("sync", lambda e, d=dst, s=src: e.dma_start(out=d, in_=s), sem, 16)

    def store(self, dst, src, sem="SP"):
        self.dsem(sem)
        self.em("scalar", lambda e, d=dst, s=src: e.dma_start(out=d, in_=s), sem, 16)


class View:
    def __init__(self, t):
        self.t, self.off = t, 0

    def take(self, cols, p=128):
        ap = self.t[0:p, self.off:self.off + cols]
        self.off += cols
        return ap


def nsplits(total, cap=512):
    out, o = [], 0
    while o < total:
        w = min(cap, total - o)
        out.append((o, w))
        o += w
    return out


def build_nc():
    nc = bass.Bass("TRN2", target_bir_lowering=False, detect_race_conditions=False)
    c = Ctx(nc)
    em, wait, load, store = c.em, c.wait, c.load, c.store

    def ein(name, shape, dt=F32):
        return nc.dram_tensor(name, shape, dt, kind="ExternalInput").ap()

    x_in = ein("x", [D, T])
    xhalo_in = ein("xhalo", [D, 3])
    w_inproj = ein("w_inproj", [D, 2 * DI])
    w_xproj = ein("w_xproj", [DI, 96])
    w_dtproj = ein("w_dtproj", [DTR, DI])
    w_outproj = ein("w_outproj", [DI, D])
    w_ffn1 = ein("w_ffn1", [D, DFF])
    w_ffn2 = ein("w_ffn2", [DFF, D])
    w_bn1 = ein("w_bn1", [D, DFF])
    w_bn2 = ein("w_bn2", [DFF, D])
    w_mod = ein("w_mod", [D, 6 * D])
    ins_small = {}
    for name, cols in [("c_vec", ND), ("ln1g", ND), ("ln1b", ND), ("ln2g", ND),
                       ("ln2b", ND), ("mconv_w", NDT * 4), ("mconv_b", NDT),
                       ("dt_bias", NDT), ("a_log", NDT * DS), ("d_vec", NDT),
                       ("bconv_w", NF * 3), ("bconv_b", NF), ("fb1", NF),
                       ("fb2", ND), ("hmask", 3), ("consts", 8), ("consts2", 16)]:
        ins_small[name] = ein(name, [128, cols])
    ident_in = ein("ident_in", [128, 128])
    edge_in = None
    x2_in = None

    out_ext = nc.dram_tensor("out", [D, T], F32, kind="ExternalOutput").ap()

    def scratch(name, shape, dt=F32):
        if DEBUG:
            return nc.dram_tensor("dbg_" + name, shape, dt, kind="ExternalOutput").ap()
        return nc.dram_tensor("spx_" + name, shape, dt).ap()

    sp_h1 = scratch("h1", [D, T])
    sp_xc = scratch("xc", [DI, T])
    sp_z = scratch("z", [DI, T])
    sp_xdbl = scratch("xdbl", [96, T])
    sp_mod = scratch("mod", [128, 48])
    sp_stat = scratch("stat", [1, 3300])
    sp_cl = scratch("cl", [128, 64])
    sp_cum = scratch("cum", [DI, T])
    sp_yl = scratch("yl", [DI, T])
    sp_state = scratch("state", [128, NDT * DS])
    sp_hh = scratch("hh", [128, DS * (TC4 + 1)])
    sp_gated = scratch("gated", [DI, T])
    sp_a1 = scratch("a1", [D, T])
    sp_ln2n = scratch("ln2n", [D, T])
    sp_mid = scratch("mid", [DFF, T])
    sp_x2 = scratch("x2", [D, T])
    sp_w2t = scratch("w2t", [128, NF * 1024], BF16)
    sp_bmid2 = scratch("bmid2", [DFF, T])
    sp_rm2 = scratch("rm2", [1, T + 8])
    dbg_w1t = (nc.dram_tensor("dbg_w1t", [128, ND * 4096], BF16,
                              kind="ExternalOutput").ap() if DEBUG else None)
    ag2_in = nc.dram_tensor("ag2_in", [1, 128 * 16], F32).ap()
    ag2_out = nc.dram_tensor("ag2_out", [2, 128 * 16], F32).ap()
    ag1_in = nc.dram_tensor("ag1_in", [1, 128 * NDT * DS], F32).ap()
    ag1_out = nc.dram_tensor("ag1_out", [2, 128 * NDT * DS], F32).ap()
    sp_row = nc.dram_tensor("sp_row", [16, 1088], F32).ap()
    _ring = [0]

    def bcast(dst, src_row, W):
        i = _ring[0] % 16
        _ring[0] += 1
        store(sp_row[i:i + 1, 0:W], src_row, sem="SR")
        wait("sync", "SR")
        c.dsem("LB")
        c.em("sync", lambda e, d=dst, s=sp_row[i:i + 1, 0:W]: e.dma_start(
            out=d, in_=s.partition_broadcast(128).squeeze(1)), "LB", 16)

    ident = nc.alloc_sbuf_tensor("ident", [128, 128], F32)
    sbp = {k: nc.alloc_sbuf_tensor("p_" + k, [128, v.shape[1]], F32)
           for k, v in ins_small.items()}
    consts = sbp["consts"]
    nc.const_aps.register(F32, 1.0, consts[:, 0:1])
    nc.const_aps.register(F32, -1.0, consts[:, 1:2])
    nc.const_aps.register(F32, 1e-5, consts[:, 2:3])
    ones_col = consts[:, 0:1]
    p_A = nc.alloc_sbuf_tensor("p_A", [128, NDT * DS], F32)
    modv = nc.alloc_sbuf_tensor("modv", [128, 48], F32)
    xdbl = nc.alloc_sbuf_tensor("xdbl", [96, T], F32)
    xpw = nc.alloc_sbuf_tensor("xpw", [128, NDT * 96], F32)
    dtpw = nc.alloc_sbuf_tensor("dtpw", [64, 2048], F32)
    scan_state = nc.alloc_sbuf_tensor("scan_state", [128, NDT * DS], F32)
    cum_state = nc.alloc_sbuf_tensor("cum_state", [128, NDT], F32)
    conv_tails = nc.alloc_sbuf_tensor("conv_tails", [128, NDT * 3], F32)
    edge = nc.alloc_sbuf_tensor("edge", [128, 20], F32)
    h0_sb = nc.alloc_sbuf_tensor("h0_sb", [128, NDT * DS], F32)
    h0g = nc.alloc_sbuf_tensor("h0g", [128, NDT * DS], F32)

    arA = nc.alloc_sbuf_tensor("arA", [128, 16448], F32)
    arB = nc.alloc_sbuf_tensor("arB", [128, 8224], F32)
    arC = nc.alloc_sbuf_tensor("arC", [128, 8224], F32)
    arD = nc.alloc_sbuf_tensor("arD", [128, 4160], F32)
    arE = nc.alloc_sbuf_tensor("arE", [128, 3264], F32)
    arF = nc.alloc_sbuf_tensor("arF", [128, 3264], F32)
    stat = arF[0:1, 0:3264]
    rem = nc.sbuf_bytes_remaining
    print("sbuf remaining bytes:", rem)
    assert rem >= 0

    pb = [nc.alloc_psum_tensor(f"pb{i}", [128, 512], F32) for i in range(8)]
    c.vec_inc_op = lambda e: e.tensor_copy(consts[:, 7:8], consts[:, 6:7])
    c.sc_inc_op = lambda e: e.copy(consts[:, 7:8], consts[:, 6:7])
    c.mm_inc_op = lambda e: e.matmul(pb[7][0:1, 0:1], consts[:, 0:1], consts[:, 0:1],
                                     start=True, stop=True)

    def p0():
        for k, src in ins_small.items():
            load(sbp[k][:], src)
        load(ident[:], ident_in)
        load(xpw[:].rearrange("p (k n) -> p k n", n=96), w_xproj.rearrange("(k p) n -> p k n", p=128))
        load(dtpw[:], w_dtproj)
        ld0 = c.v("L0")
        wait("scalar", "L0", ld0)
        em("scalar", lambda e: e.activation(p_A[:], sbp["a_log"][:], AF.Exp))
        em("scalar", lambda e: e.mul(p_A[:], p_A[:], -1.0), "SC")
        vE = View(arE)
        silu_c = vE.take(ND)
        em("scalar", lambda e, o=silu_c: e.activation(o, sbp["c_vec"][:], AF.Silu), "SC")
        sc0 = c.v("SC")
        wt = [vE.take(128) for _ in range(4)]
        mm_at = {}
        ev_at = {}
        for m in range(48):
            if m >= 2:
                wait("tensor", "VV", ev_at[m - 2])
            for k in range(ND):
                i = m * ND + k
                buf = wt[i % 4]
                lw = "LW%d" % (i % 4)
                if i >= 4:
                    wait("sync", "MM", mm_at[i - 4])
                load(buf, w_mod[k * 128:(k + 1) * 128, m * 128:(m + 1) * 128], sem=lw)
                wait("tensor", lw)
                if i == 0:
                    wait("tensor", "SC", sc0)
                em("tensor", lambda e, b=buf, kk=k, m=m: e.matmul(
                    pb[m % 2][:, 0:1], b, silu_c[:, kk:kk + 1],
                    start=(kk == 0), stop=(kk == ND - 1)), "MM")
                mm_at[i] = c.v("MM")
            wait("vector", "MM", mm_at[m * ND + ND - 1])
            em("vector", lambda e, m_=m: e.tensor_copy(modv[:, m_:m_ + 1], pb[m_ % 2][:, 0:1]), "VV")
            ev_at[m] = c.v("VV")
        ones8 = sbp["consts2"][:, 0:8]
        q8 = sbp["consts2"][:, 8:16]
        bufsets = [[vE.take(ND) for _ in range(7)] for _ in range(2)]
        for bi, base in enumerate((16, 40)):
            g = modv[:, base:base + ND]
            b1, b2, b3, b4, b5, b6, b7 = bufsets[bi]
            wait("scalar", "VV")
            em("scalar", lambda e, g=g, o=b1: e.activation(o, g, AF.Sign))
            em("scalar", lambda e, i=b1, o=b2: e.activation(o, i, AF.Square), "SC")
            wait("vector", "SC")
            em("vector", lambda e, a=b1, b=b2, o=b3: e.tensor_sub(o, a, b), "VV")
            wait("vector", "VV")
            em("vector", lambda e, i=b3, o=b4: e.tensor_add(o, i, ones8), "VV")
            wait("vector", "VV")
            em("vector", lambda e, g=g, s=b4, o=b5: e.tensor_mul(o, g, s), "VV")
            wait("vector", "VV")
            em("vector", lambda e, i=b5, o=b6: e.tensor_max(o, i, q8), "VV")
            wait("vector", "VV")
            em("vector", lambda e, s=b4, m=b6, o=b7: e.tensor_mul(o, s, m), "VV")
            wait("vector", "VV")
            em("vector", lambda e, o=b7, g=g: e.tensor_copy(g, o), "VV")
            if DEBUG and bi == 0:
                c.barrier()
                for di_, b in enumerate(bufsets[0]):
                    store(sp_cl[:, di_ * 8:di_ * 8 + 8], b)
                store(sp_cl[:, 56:64], ones8)
                c.barrier()
        for base in (8, 32):
            g = modv[:, base:base + ND]
            em("vector", lambda e, g=g: e.tensor_single_scalar(g, g, 1.0, ALU.add), "VV")
        if DEBUG:
            wait("scalar", "VV")
            store(sp_mod, modv[:])

    def emit_ln(in_tiles, out_tiles, W, scale_cols, bias_cols, pbank, sq, rep_a, rep_b):
        n_in = len(in_tiles)
        mmv = {}
        gq = [0]
        srow = stat[0:1, 0:W]
        srow2 = stat[0:1, 1088:1088 + W]
        mrow = stat[0:1, 2176:2176 + W]
        for o, w in nsplits(W):
            wait("tensor", "VV")
            if not DEV_NOMM:
                for i, t in enumerate(in_tiles):
                    em("tensor", lambda e, t=t, o=o, w=w, i=i: e.matmul(
                        pbank[0:1, 0:w], ones_col, t[:, o:o + w],
                        start=(i == 0), stop=(i == n_in - 1)),
                        "MM" if i == n_in - 1 else None)
            if DEV_NOEVAC:
                continue
            wait("vector", "MM")
            em("vector", lambda e, o=o, w=w: e.tensor_copy(srow[0:1, o:o + w], pbank[0:1, 0:w]), "VV")
            if DEV_SUB < 2:
                continue
            wait("tensor", "VV")
            for i, t in enumerate(in_tiles):
                gi = gq[0]
                gq[0] += 1
                s = sq[gi % 2]
                if gi >= 2:
                    wait("scalar", "MM", mmv[gi - 2])
                em("scalar", lambda e, t=t, o=o, w=w, s=s: e.activation(
                    s[:, 0:w], t[:, o:o + w], AF.Square), "SC")
                wait("tensor", "SC")
                em("tensor", lambda e, w=w, i=i, s=s: e.matmul(
                    pbank[0:1, 0:w], ones_col, s[:, 0:w],
                    start=(i == 0), stop=(i == n_in - 1)), "MM")
                mmv[gi] = c.v("MM")
            wait("vector", "MM")
            em("vector", lambda e, o=o, w=w: e.tensor_copy(srow2[0:1, o:o + w], pbank[0:1, 0:w]), "VV")
        if DEV_SUB < 3:
            return
        Dm = float(128 * n_in)
        em("vector", lambda e: e.tensor_single_scalar(srow, srow, 1.0 / Dm, ALU.mult))
        em("vector", lambda e: e.tensor_single_scalar(srow2, srow2, 1.0 / Dm, ALU.mult))
        em("vector", lambda e: e.tensor_mul(mrow, srow, srow))
        em("vector", lambda e: e.tensor_sub(srow2, srow2, mrow), "VV")
        wait("scalar", "VV")
        em("scalar", lambda e: e.activation(srow2, srow2, AF.Sqrt, bias=1e-5), "SC")
        wait("vector", "SC")
        em("vector", lambda e: e.reciprocal(srow2, srow2))
        em("vector", lambda e: e.tensor_mul(mrow, srow, srow2), "VV")
        wait("scalar", "VV")
        bcast(rep_a[:, 0:W], srow2, W)
        bcast(rep_b[:, 0:W], mrow, W)
        wait("vector", "LB")
        if DEV_SUB < 9:
            return
        for i, (ti, to) in enumerate(zip(in_tiles, out_tiles)):
            s_ = scale_cols[i] if scale_cols is not None else None
            b_ = bias_cols[i] if bias_cols is not None else None
            if s_ is None:
                em("vector", lambda e, ti=ti, to=to: e.tensor_mul(to[:, 0:W], ti[:, 0:W], rep_a[:, 0:W]))
                em("vector", lambda e, to=to: e.tensor_sub(to[:, 0:W], to[:, 0:W], rep_b[:, 0:W]), "VV")
            else:
                em("vector", lambda e, s=s_: e.tensor_scalar(
                    sq[0][:, 0:W], rep_a[:, 0:W], s, None, ALU.mult))
                em("vector", lambda e, s=s_, b=b_: e.tensor_scalar(
                    sq[1][:, 0:W], rep_b[:, 0:W], s, b, ALU.mult, ALU.subtract))
                em("vector", lambda e, ti=ti, to=to: e.tensor_mul(to[:, 0:W], ti[:, 0:W], sq[0][:, 0:W]))
                em("vector", lambda e, to=to: e.tensor_sub(to[:, 0:W], to[:, 0:W], sq[1][:, 0:W]), "VV")

    def a12():
        for ci in range(DEV_NCHUNKS):
            if ci == 0 or DEV_C2SUB >= 2:
                c.barrier()
            g0 = ci * TC1
            halo = 3 if ci == 0 else 0
            W = TC1 + halo
            vA, vB, vC, vD, vE, vF = View(arA), View(arB), View(arC), View(arD), View(arE), View(arF)
            xt = [vA.take(W) for _ in range(ND)]
            h1t = [vA.take(W) for _ in range(ND)]
            h1nt = [vC.take(W) for _ in range(ND)]
            sq = [vD.take(W), vD.take(W)]
            rep_a, rep_b = vD.take(W), vD.take(W)
            cin = [vB.take(TC1 + 3), vB.take(TC1 + 3)]
            xc_t = vB.take(TC1)
            zt = vB.take(W)
            wbuf = [vE.take(8 * 128), vE.take(8 * 128)]
            lx = "LX%d" % (ci % 2)
            for i in range(ND):
                if halo:
                    load(xt[i][:, 0:3], xhalo_in[i * 128:(i + 1) * 128, :], sem=lx)
                    load(xt[i][:, 3:W], x_in[i * 128:(i + 1) * 128, g0:g0 + TC1], sem=lx)
                else:
                    load(xt[i], x_in[i * 128:(i + 1) * 128, g0:g0 + TC1], sem=lx)
            ldx = c.v(lx)
            for eng in ("tensor", "scalar", "vector"):
                wait(eng, lx, ldx)
            if ci > 0 and DEV_C2SUB < 1:
                continue
            emit_ln(xt, h1t, W,
                    [modv[:, 8 + i:9 + i] for i in range(ND)],
                    [modv[:, 0 + i:1 + i] for i in range(ND)],
                    pb[7], sq, rep_a, rep_b)
            wait("scalar", "VV")
            if DEV_SUB < 9:
                continue
            if DEBUG and ci == 0:
                for so in (0, 1088, 2176):
                    store(sp_stat[0:1, so:so + 1027], stat[0:1, so:so + 1027])
            for i in range(ND):
                store(sp_h1[i * 128:(i + 1) * 128, g0:g0 + TC1], h1t[i][:, halo:W])
            if DEV_STAGE < 2:
                continue
            emit_ln(h1t, h1nt, W,
                    [sbp["ln1g"][:, i:i + 1] for i in range(ND)],
                    [sbp["ln1b"][:, i:i + 1] for i in range(ND)],
                    pb[7], sq, rep_a, rep_b)
            if DEV_STAGE < 3:
                continue
            rhs_ready = c.v("VV")
            mm_after_f = {}
            evg = []
            for f in range(NF):
                wb = wbuf[f % 2].rearrange("p (k n) -> p k n", k=ND)
                lw = "LW%d" % (f % 2)
                if f >= 2:
                    wait("sync", "MM", mm_after_f[f - 2])
                load(wb, w_inproj.rearrange("(k p) n -> p k n", p=128)[:, :, f * 128:(f + 1) * 128], sem=lw)
                wld = c.v(lw)
                di = f if f < NDT else f - NDT
                first = True
                for o, w in nsplits(W):
                    g = len(evg)
                    bank = pb[g % 2]
                    if g >= 2:
                        wait("tensor", evg[g - 2][0], evg[g - 2][1])
                    for k in range(ND):
                        if first:
                            wait("tensor", lw, wld)
                            wait("tensor", "VV", rhs_ready)
                            first = False
                        em("tensor", lambda e, b=bank, wb=wb, k=k, o=o, w=w, rh=h1nt[k]: e.matmul(
                            b[:, 0:w], wb[:, k, :], rh[:, o:o + w],
                            start=(k == 0), stop=(k == ND - 1)),
                            "MM" if k == ND - 1 else None)
                    if f < NDT:
                        cb = cin[f % 2]
                        dst_o = o if halo else 3 + o
                        wait("scalar", "MM")
                        em("scalar", lambda e, cb=cb, o=dst_o, w=w, b=bank: e.copy(
                            cb[:, o:o + w], b[:, 0:w]), "SC")
                        evg.append(("SC", c.v("SC")))
                    else:
                        wait("scalar", "MM")
                        em("scalar", lambda e, o=o, w=w, b=bank, z=zt: e.activation(
                            z[:, o:o + w], b[:, 0:w], AF.Silu), "SC")
                        evg.append(("SC", c.v("SC")))
                mm_after_f[f] = c.v("MM")
                if f < NDT:
                    cb = cin[f % 2]
                    wait("vector", "SC")
                    if halo:
                        em("vector", lambda e, cb=cb: e.tensor_mul(
                            cb[:, 0:3], cb[:, 0:3], sbp["hmask"][:]), "VV")
                    else:
                        em("vector", lambda e, cb=cb, di=di: e.tensor_copy(
                            cb[:, 0:3], conv_tails[:, di * 3:di * 3 + 3]), "VV")
                    em("vector", lambda e, cb=cb, di=di: e.tensor_copy(
                        conv_tails[:, di * 3:di * 3 + 3], cb[:, TC1:TC1 + 3]), "VV")
                    wait("vector", "SP")
                    wait("vector", "MM")
                    wait("vector", "VV")
                    w0 = sbp["mconv_w"][:, di * 4 + 0:di * 4 + 1]
                    em("vector", lambda e, cb=cb, s=w0, xc=xc_t: e.tensor_scalar(
                        xc[:], cb[:, 0:TC1], s, None, ALU.mult))
                    for j in (1, 2, 3):
                        wj = sbp["mconv_w"][:, di * 4 + j:di * 4 + j + 1]
                        em("vector", lambda e, cb=cb, s=wj, j=j, xc=xc_t: e.scalar_tensor_tensor(
                            xc[:], cb[:, j:j + TC1], s, xc[:], ALU.mult, ALU.add),
                            "VV" if j == 3 else None)
                    wait("scalar", "VV")
                    em("scalar", lambda e, di=di, xc=xc_t: e.activation(
                        xc[:], xc[:], AF.Silu,
                        bias=sbp["mconv_b"][:, di:di + 1]), "SC")
                    wait("scalar", "SC")
                    store(sp_xc[di * 128:(di + 1) * 128, g0:g0 + TC1], xc_t[:])
                    xp_ready = c.v("SC")
                    for oi, (o, w) in enumerate(nsplits(TC1)):
                        wait("tensor", "SC", xp_ready)
                        em("tensor", lambda e, di=di, oi=oi, o=o, w=w, xc=xc_t: e.matmul(
                            pb[2 + oi][0:96, 0:w], xpw[:, di * 96:(di + 1) * 96],
                            xc[:, o:o + w],
                            start=(di == 0), stop=(di == NDT - 1)), "MM")
                    mm_after_f[f] = c.v("MM")
                else:
                    wait("scalar", "SC")
                    store(sp_z[di * 128:(di + 1) * 128, g0:g0 + TC1], zt[:, halo:W])
            wait("vector", "MM")
            for oi, (o, w) in enumerate(nsplits(TC1)):
                em("vector", lambda e, oi=oi, o=o, w=w, g0=g0: e.tensor_copy(
                    xdbl[:, g0 + o:g0 + o + w], pb[2 + oi][0:96, 0:w]), "VV")
        if DEV_STAGE >= 3:
            wait("scalar", "VV")
            store(sp_xdbl[:, 0:DEV_NCHUNKS * TC1], xdbl[:, 0:DEV_NCHUNKS * TC1])

    def a4():
        c.barrier()
        vA, vB, vC, vD, vE, vF = View(arA), View(arB), View(arC), View(arD), View(arE), View(arF)
        SW = DS * (TC4 + 1)
        HW_ = SW // 2
        HC = 8 * TC4
        da = [vA.take(SW), vA.take(SW)]
        bb = [vB.take(SW), vB.take(SW)]
        hh = vC.take(SW)
        C_rep = [vC.take(DS * TC4), vA.take(DS * TC4)]
        B_rep = [vD.take(DS * TC4), vA.take(DS * TC4)]
        zeros = vE.take(TC4 + 1)
        cum_in = [vE.take(TC4 + 1), vE.take(TC4 + 1)]
        cum_out = [vE.take(TC4 + 1), vE.take(TC4 + 1)]
        relu_t = vE.take(TC4)
        sp_ln = vE.take(TC4)
        xc_t = [vE.take(TC4), vE.take(TC4)]
        u_t = [vE.take(TC4), vE.take(TC4)]
        yl_t = vE.take(TC4)
        yl2b = [arF[:, 2688:2688 + TC4], arF[:, 2688 + TC4:2688 + 2 * TC4]]
        em("vector", lambda e: e.memset(zeros[:], 0.0))
        for p in range(2):
            em("vector", lambda e, p=p: e.memset(
                da[p].rearrange("q (s t) -> q s t", t=TC4 + 1)[:, :, 0:1], 0.0))
        em("gpsimd", lambda e: e.memset(cum_state[:], 0.0))
        em("gpsimd", lambda e: e.memset(scan_state[:], 0.0), "GP")
        em("vector", lambda e: e.nop(), "VV")
        wait("gpsimd", "VV")
        wait("vector", "GP")
        NU = NC4 * NDT
        spv, addv, cumv, updv, uv, xdv, dav, bv = {}, {}, {}, {}, {}, {}, {}, {}
        col0v, bc0v = {}, {}
        sl_v, sh_v, glv, ghv, redv, mmv = {}, {}, {}, {}, {}, {}
        spc_v, spy_v, lcv = {}, {}, {}
        lbc_v = {}

        def emit_bc_loads(cj):
            par = cj % 2
            for s in range(DS):
                c.dsem("LBc")
                c.em("scalar", lambda e, s=s, cj=cj, par=par: e.dma_start(
                    out=B_rep[par][:, s * TC4:(s + 1) * TC4],
                    in_=sp_xdbl[64 + s:65 + s, cj * TC4:(cj + 1) * TC4].partition_broadcast(128).squeeze(1)),
                    "LBc", 16)
                c.em("sync", lambda e, s=s, cj=cj, par=par: e.dma_start(
                    out=C_rep[par][:, s * TC4:(s + 1) * TC4],
                    in_=sp_xdbl[80 + s:81 + s, cj * TC4:(cj + 1) * TC4].partition_broadcast(128).squeeze(1)),
                    "LBc", 16)
            lbc_v[cj] = c.v("LBc")

        emit_bc_loads(0)
        emit_bc_loads(1)
        hh3 = hh.rearrange("q (s t) -> q s t", t=TC4 + 1)
        for u in range(NU + 6):
            if 0 <= u - 3 < NU:
                n = u - 3
                di = n % NDT
                par = n % 2
                wait("scalar", "GP", addv[n])
                if n >= 2:
                    wait("scalar", "VV", sh_v[n - 2])
                for s in range(DS):
                    em("scalar", lambda e, s=s, di=di, par=par: e.activation(
                        da[par][:, s * (TC4 + 1) + 1:(s + 1) * (TC4 + 1)],
                        cum_in[par][:, 1:TC4 + 1], AF.Exp,
                        scale=p_A[:, di * DS + s:di * DS + s + 1]),
                        "SC" if s == DS - 1 else None)
                dav[n] = c.v("SC")
            if 0 <= u - 1 < NU:
                n = u - 1
                di = n % NDT
                par = n % 2
                wait("gpsimd", "SC", spv[n])
                if n >= 2:
                    wait("gpsimd", "SC", dav[n - 2])
                em("gpsimd", lambda e, par=par: e.tensor_add(
                    cum_in[par][:, 1:TC4 + 1], relu_t[:], sp_ln[:]), "GP")
                addv[n] = c.v("GP")
                em("scalar", lambda e, par=par, di=di: e.copy(
                    cum_in[par][:, 0:1], cum_state[:, di:di + 1]), "SC")
                col0v[n] = c.v("SC")
            if 0 <= u - 2 < NU:
                n = u - 2
                ci, di = n // NDT, n % NDT
                par = n % 2
                wait("vector", "GP", addv[n])
                wait("vector", "SC", col0v[n])
                if n >= 2:
                    wait("vector", "SC", updv[n - 2])
                    wait("vector", "SPc", spc_v[n - 2])
                em("vector", lambda e, par=par: e.tensor_tensor_scan(
                    cum_out[par][:], cum_in[par][:], zeros[:], 0.0, ALU.add, ALU.add), "VV")
                cumv[n] = c.v("VV")
                wait("scalar", "VV", cumv[n])
                em("scalar", lambda e, par=par, di=di: e.copy(
                    cum_state[:, di:di + 1], cum_out[par][:, TC4:TC4 + 1]), "SC")
                updv[n] = c.v("SC")
                wait("vector", lcv[n][0], lcv[n][1])
                if n >= 2:
                    wait("vector", "GP", bv[n - 2])
                em("vector", lambda e, par=par: e.tensor_mul(
                    u_t[par][:], cum_in[par][:, 1:TC4 + 1], xc_t[par][:]), "VV")
                uv[n] = c.v("VV")
                wait("scalar", "VV", cumv[n])
                store(sp_cum[di * 128:(di + 1) * 128, ci * TC4:(ci + 1) * TC4],
                      cum_out[par][:, 1:TC4 + 1], sem="SPc")
                spc_v[n] = c.v("SPc")
            if 0 <= u - 5 < NU:
                n = u - 5
                ci, di = n // NDT, n % NDT
                par = n % 2
                wait("vector", "GP", ghv[n])
                em("vector", lambda e: e.tensor_reduce(
                    yl_t[:], hh3[:, :, 1:TC4 + 1].transpose([0, 2, 1]), AX.X, ALU.add), "VV")
                wait("vector", "VV")
                em("vector", lambda e, par=par: e.tensor_add(
                    yl2b[par][:], yl2b[par][:], yl_t[:]), "VV")
                redv[n] = c.v("VV")
                wait("scalar", "VV", redv[n])
                store(sp_yl[di * 128:(di + 1) * 128, ci * TC4:(ci + 1) * TC4],
                      yl2b[par][:], sem="SPy")
                spy_v[n] = c.v("SPy")
                if di == NDT - 1 and ci + 2 < NC4:
                    wait("sync", "GP", ghv[n])
                    wait("scalar", "GP", ghv[n])
                    emit_bc_loads(ci + 2)
            if 0 <= u - 3 < NU:
                n = u - 3
                ci = n // NDT
                di = n % NDT
                par = n % 2
                if n >= 2:
                    wait("scalar", "VV", sh_v[n - 2])
                em("scalar", lambda e, par=par, di=di: e.copy(
                    bb[par].rearrange("q (s t) -> q s t", t=TC4 + 1)[:, :, 0:1],
                    scan_state[:, di * DS:(di + 1) * DS].unsqueeze(2)), "SC")
                bc0v[n] = c.v("SC")
                wait("gpsimd", "VV", uv[n])
                wait("gpsimd", "LBc", lbc_v[ci])
                if n >= 2:
                    wait("gpsimd", "VV", sh_v[n - 2])
                em("gpsimd", lambda e, par=par, cpar=ci % 2: e.tensor_tensor(
                    bb[par].rearrange("q (s t) -> q s t", t=TC4 + 1)[:, :, 1:TC4 + 1],
                    u_t[par][:].unsqueeze(1).broadcast_to([128, DS, TC4]),
                    B_rep[cpar][:].rearrange("q (s t) -> q s t", t=TC4),
                    ALU.mult), "GP")
                bv[n] = c.v("GP")
                if n >= 2:
                    wait("vector", "SPy", spy_v[n - 2])
                wait("vector", "VV", uv[n])
                em("vector", lambda e, par=par, di=di: e.tensor_scalar(
                    yl2b[par][:], xc_t[par][:], sbp["d_vec"][:, di:di + 1], None, ALU.mult), "VV")
                xdv[n] = c.v("VV")
            if 0 <= u - 1 < NU:
                n = u - 1
                ci, di = n // NDT, n % NDT
                par = n % 2
                lc = "LC%d" % par
                c.dsem(lc)
                if n >= 2:
                    wait("sync", "VV", xdv[n - 2])
                c.em("sync", lambda e, di=di, ci=ci, xt_=xc_t[par]: e.dma_start(
                    out=xt_, in_=sp_xc[di * 128:(di + 1) * 128, ci * TC4:(ci + 1) * TC4]), lc, 16)
                lcv[n] = (lc, c.v(lc))
            if 0 <= u - 4 < NU:
                n = u - 4
                di = n % NDT
                par = n % 2
                cpar = (n // NDT) % 2
                wait("vector", "SC", dav[n])
                wait("vector", "SC", bc0v[n])
                wait("vector", "GP", bv[n])
                em("vector", lambda e, par=par: e.tensor_tensor_scan(
                    hh[:, 0:HW_], da[par][:, 0:HW_], bb[par][:, 0:HW_],
                    0.0, ALU.mult, ALU.add), "VV")
                sl_v[n] = c.v("VV")
                wait("scalar", "VV", sl_v[n])
                em("scalar", lambda e, di=di: e.copy(
                    scan_state[:, di * DS:di * DS + 8].unsqueeze(2),
                    hh3[:, 0:8, TC4:TC4 + 1]), "SC")
                ulo = c.v("SC")
                wait("gpsimd", "SC", ulo)
                em("gpsimd", lambda e, cpar=cpar: e.tensor_tensor(
                    hh3[:, 0:8, 1:TC4 + 1], hh3[:, 0:8, 1:TC4 + 1],
                    C_rep[cpar][:].rearrange("q (s t) -> q s t", t=TC4)[:, 0:8, :],
                    ALU.mult), "GP")
                glv[n] = c.v("GP")
                em("vector", lambda e, par=par: e.tensor_tensor_scan(
                    hh[:, HW_:SW], da[par][:, HW_:SW], bb[par][:, HW_:SW],
                    0.0, ALU.mult, ALU.add), "VV")
                sh_v[n] = c.v("VV")
                wait("scalar", "VV", sh_v[n])
                em("scalar", lambda e, di=di: e.copy(
                    scan_state[:, di * DS + 8:(di + 1) * DS].unsqueeze(2),
                    hh3[:, 8:16, TC4:TC4 + 1]), "SC")
                uhi = c.v("SC")
                wait("gpsimd", "SC", uhi)
                em("gpsimd", lambda e, cpar=cpar: e.tensor_tensor(
                    hh3[:, 8:16, 1:TC4 + 1], hh3[:, 8:16, 1:TC4 + 1],
                    C_rep[cpar][:].rearrange("q (s t) -> q s t", t=TC4)[:, 8:16, :],
                    ALU.mult), "GP")
                ghv[n] = c.v("GP")
            if u < NU:
                n = u
                ci, di = n // NDT, n % NDT
                par = n % 2
                if n >= 1:
                    wait("tensor", "SC", spv[n - 1])
                em("tensor", lambda e, di=di, ci=ci: e.matmul(
                    pb[4][:, 0:TC4],
                    dtpw[:, di * 128:(di + 1) * 128],
                    xdbl[0:64, ci * TC4:(ci + 1) * TC4], start=True, stop=True), "MM")
                mmv[n] = c.v("MM")
                wait("scalar", "MM", mmv[n])
                if n >= 1:
                    wait("scalar", "GP", addv[n - 1])
                em("scalar", lambda e, di=di: e.activation(
                    sp_ln[:], pb[4][:, 0:TC4], AF.Abs, bias=sbp["dt_bias"][:, di:di + 1]))
                em("scalar", lambda e: e.activation(sp_ln[:], sp_ln[:], AF.Exp, scale=-1.0))
                em("scalar", lambda e: e.activation(sp_ln[:], sp_ln[:], AF.Ln, bias=1.0))
                em("scalar", lambda e, di=di: e.activation(
                    relu_t[:], pb[4][:, 0:TC4], AF.Relu, bias=sbp["dt_bias"][:, di:di + 1]), "SC")
                spv[n] = c.v("SC")
        if DEBUG:
            c.barrier()
            store(sp_state, scan_state[:])

    def b1():
        TCB1 = 512
        NB1 = T // TCB1
        c.barrier()
        c.dsem("GD")
        em("gpsimd", lambda e: e.dma_start(
            out=ag1_in.rearrange("o (p n) -> (o p) n", p=128), in_=scan_state[:]), "GD", 16)
        wait("gpsimd", "GD")
        groups = [[0, 1], [2, 3], [4, 5], [6, 7]]
        em("gpsimd", lambda e, groups=groups: e.collective_compute(
            "AllGather", ALU.bypass, replica_groups=groups,
            ins=[ag1_in.opt()], outs=[ag1_out.opt()]), "CC")
        wait("sync", "CC")
        c.dsem("LH")
        c.em("sync", lambda e: e.dma_start(
            out=h0_sb[:], in_=ag1_out[0:1, :].rearrange("o (p n) -> (o p) n", p=128)), "LH", 16)
        wait("scalar", "LH")
        em("scalar", lambda e: e.activation(h0g[:], h0_sb[:], AF.Copy,
                                            scale=consts[:, 3:4]), "SC")
        h0_ready = c.v("SC")
        HP = 8 * TCB1
        vA, vB, vC, vD, vE = View(arA), View(arB), View(arC), View(arD), View(arE)
        pa4 = [vA.take(HP) for _ in range(4)]
        crep = [vB.take(HP), vB.take(HP), vC.take(HP), vC.take(HP)]
        cum_t = [vD.take(TCB1) for _ in range(3)]
        gat_t = [vD.take(TCB1) for _ in range(2)]
        ylr_t = [vE.take(TCB1), vE.take(TCB1)]
        zr_t = [vE.take(TCB1), vE.take(TCB1)]
        yacc = [vE.take(TCB1), vE.take(TCB1)]
        NU = NB1 * NDT * 2
        pv, h0v, cv, t1v, fv = {}, {}, {}, {}, {}
        ldc, ldy, lbc_v = {}, {}, {}

        def parts(u):
            ci, r = divmod(u, NDT * 2)
            di, hf = divmod(r, 2)
            return ci, di, hf

        def emit_cload(hu):
            ci, hf = hu // 2, hu % 2
            buf = crep[hu % 4]
            if hu >= 4:
                pci, phf = (hu - 4) // 2, (hu - 4) % 2
                w_last = pci * (NDT * 2) + (NDT - 1) * 2 + phf
                wait("sync", "VV", cv[w_last])
                wait("scalar", "VV", cv[w_last])
            for s in range(8):
                sg = hf * 8 + s
                c.dsem("LBc")
                c.em("sync" if s % 2 == 0 else "scalar",
                     lambda e, buf=buf, s=s, sg=sg, ci=ci: e.dma_start(
                         out=buf[:, s * TCB1:(s + 1) * TCB1],
                         in_=sp_xdbl[80 + sg:81 + sg, ci * TCB1:(ci + 1) * TCB1]
                         .partition_broadcast(128).squeeze(1)), "LBc", 16)
            lbc_v[hu] = c.v("LBc")

        wait("gpsimd", "SC", h0_ready)
        wait("scalar", "SC", h0_ready)
        c.dsem("SPg")
        emit_cload(0)
        emit_cload(1)
        for u in range(NU + 4):
            if 0 <= u - 2 < NU:
                w = u - 2
                ci, di, hf = parts(w)
                hu = ci * 2 + hf
                pw = pa4[w % 4]
                cw = crep[hu % 4]
                wait("vector", "GP", h0v[w])
                wait("vector", "LBc", lbc_v[hu])
                em("vector", lambda e, pw=pw, cw=cw: e.tensor_tensor(
                    pw[:, 0:HP], pw[:, 0:HP], cw[:, 0:HP], ALU.mult), "VV")
                cv[w] = c.v("VV")
                if hf == 0:
                    c.dsem("LDy")
                    key2 = ci * NDT + di
                    if key2 >= 2:
                        wait("sync", "VV", fv[2 * (key2 - 2) + 1])
                    load(ylr_t[di % 2], sp_yl[di * 128:(di + 1) * 128, ci * TCB1:(ci + 1) * TCB1], sem="LDy")
                    load(zr_t[di % 2], sp_z[di * 128:(di + 1) * 128, ci * TCB1:(ci + 1) * TCB1], sem="LDy")
                    ldy[(ci, di)] = c.v("LDy")
            if 0 <= u - 2 < NU:
                w = u - 2
                ciw, diw, hfw = parts(w)
                if diw == 2 and hfw == 0:
                    for nxt in ((ciw + 1) * 2, (ciw + 1) * 2 + 1):
                        if nxt < NB1 * 2 and nxt not in lbc_v:
                            emit_cload(nxt)
            if 0 <= u - 3 < NU:
                w = u - 3
                pw = pa4[w % 4]
                H4 = 4 * TCB1
                if w % 2 == 0:
                    wait("gpsimd", "VV", cv[w])
                    em("gpsimd", lambda e, pw=pw, H4=H4: e.tensor_tensor(
                        pw[:, 0:H4], pw[:, 0:H4], pw[:, H4:2 * H4], ALU.add), "GP")
                    t1v[w] = ("GP", c.v("GP"))
                else:
                    wait("vector", "VV", cv[w])
                    em("vector", lambda e, pw=pw, H4=H4: e.tensor_tensor(
                        pw[:, 0:H4], pw[:, 0:H4], pw[:, H4:2 * H4], ALU.add), "VV")
                    t1v[w] = ("VV", c.v("VV"))
            if 0 <= u - 4 < NU:
                w = u - 4
                ci, di, hf = parts(w)
                pw = pa4[w % 4]
                sem_n, sem_v = t1v[w]
                wait("vector", sem_n, sem_v)
                if hf == 0:
                    em("vector", lambda e, pw=pw, ya=yacc[di % 2]: e.tensor_reduce(
                        ya[:], pw.rearrange("q (s t) -> q s t", t=TCB1)[:, 0:4, :]
                        .transpose([0, 2, 1]), AX.X, ALU.add), "VV")
                else:
                    wait("vector", "LDy", ldy[(ci, di)])
                    wait("vector", "SPg")
                    g_ = gat_t[di % 2]
                    em("vector", lambda e, pw=pw, g_=g_: e.tensor_reduce(
                        g_[:], pw.rearrange("q (s t) -> q s t", t=TCB1)[:, 0:4, :]
                        .transpose([0, 2, 1]), AX.X, ALU.add), "VV")
                    wait("vector", "VV")
                    em("vector", lambda e, g_=g_, ya=yacc[di % 2]: e.tensor_add(
                        g_[:], g_[:], ya[:]), "VV")
                    wait("vector", "VV")
                    em("vector", lambda e, g_=g_, yl=ylr_t[di % 2]: e.tensor_add(
                        g_[:], g_[:], yl[:]), "VV")
                    wait("vector", "VV")
                    em("vector", lambda e, g_=g_, zr=zr_t[di % 2]: e.tensor_tensor(
                        g_[:], g_[:], zr[:], ALU.mult), "VV")
                    gv_ = c.v("VV")
                    wait("scalar", "VV", gv_)
                    store(sp_gated[di * 128:(di + 1) * 128, ci * TCB1:(ci + 1) * TCB1],
                          gat_t[di % 2][:], sem="SPg")
                fv[w] = c.v("VV")
            if u < NU:
                ci, di, hf = parts(u)
                j = u % 4
                if hf == 0:
                    c.dsem("LDc")
                    key = ci * NDT + di
                    if key >= 3:
                        wait("sync", "SC", pv.get(2 * (key - 3) + 1, 0))
                    load(cum_t[key % 3], sp_cum[di * 128:(di + 1) * 128, ci * TCB1:(ci + 1) * TCB1], sem="LDc")
                    ldc[key] = c.v("LDc")
                if u >= 4:
                    wait("scalar", "VV", fv[u - 4])
                wait("scalar", "LDc", ldc[ci * NDT + di])
                for s in range(8):
                    sg = hf * 8 + s
                    em("scalar", lambda e, s=s, sg=sg, di=di, j=j, ct=cum_t[(ci * NDT + di) % 3]: e.activation(
                        pa4[j][:, s * TCB1:(s + 1) * TCB1], ct[:],
                        AF.Exp, scale=p_A[:, di * DS + sg:di * DS + sg + 1]),
                        "SC" if s == 7 else None)
                pv[u] = c.v("SC")
            if 0 <= u - 1 < NU:
                w = u - 1
                ci, di, hf = parts(w)
                pw = pa4[w % 4].rearrange("q (s t) -> q s t", t=TCB1)
                base = di * DS + hf * 8
                wait("gpsimd", "SC", pv[w])
                em("gpsimd", lambda e, pw=pw, base=base: e.tensor_tensor(
                    pw, pw,
                    h0g[:, base:base + 8].unsqueeze(2).broadcast_to([128, 8, TCB1]),
                    ALU.mult), "GP")
                h0v[w] = c.v("GP")

    def b2a():
        c.barrier()
        vA = View(arA)
        opw = vA.take(NDT * 1024)
        load(opw.rearrange("p (k m) -> p k m", m=1024),
             w_outproj.rearrange("(k p) m -> p k m", p=128), sem="LW0")
        for ti in range(NCB):
            g0 = ti * TCB
            c.barrier()
            vB, vC, vD, vE = View(arB), View(arC), View(arD), View(arE)
            gt = [vB.take(TCB) for _ in range(NDT)]
            h1t = [vC.take(TCB) for _ in range(ND)]
            a1t = [vC.take(TCB) for _ in range(ND)]
            l2t = [vD.take(TCB) for _ in range(ND)]
            sq = [vE.take(TCB), vE.take(TCB)]
            rep_a, rep_b = vE.take(TCB), vE.take(TCB)
            for k in range(NDT):
                load(gt[k], sp_gated[k * 128:(k + 1) * 128, g0:g0 + TCB], sem="LX0")
            for m in range(ND):
                load(h1t[m], sp_h1[m * 128:(m + 1) * 128, g0:g0 + TCB], sem="LX0")
            lxv = c.v("LX0")
            wait("tensor", "LX0", lxv)
            wait("tensor", "LW0")
            for m in range(ND):
                for k in range(NDT):
                    em("tensor", lambda e, m=m, k=k, g_=gt[k]: e.matmul(
                        pb[m][:, 0:TCB], opw.rearrange("p (k m) -> p k m", m=1024)[:, k, m * 128:(m + 1) * 128],
                        g_[:], start=(k == 0), stop=(k == NDT - 1)),
                        "MM" if k == NDT - 1 else None)
            mmv = c.v("MM")
            wait("vector", "MM", mmv)
            wait("vector", "LX0", lxv)
            for m in range(ND):
                em("vector", lambda e, m=m, a_=a1t[m], h_=h1t[m]: e.tensor_add(a_[:], pb[m][:, 0:TCB], h_[:]), "VV")
            a1v = c.v("VV")
            wait("scalar", "VV", a1v)
            for m in range(ND):
                store(sp_a1[m * 128:(m + 1) * 128, g0:g0 + TCB], a1t[m][:], sem="SPa")
            emit_ln(a1t, l2t, TCB,
                    [sbp["ln2g"][:, i:i + 1] for i in range(ND)],
                    [sbp["ln2b"][:, i:i + 1] for i in range(ND)],
                    pb[7], sq, rep_a, rep_b)
            wait("scalar", "VV")
            for m in range(ND):
                store(sp_ln2n[m * 128:(m + 1) * 128, g0:g0 + TCB], l2t[m][:], sem="SPa")

    def b2b():
        for hf in range(2):
            c.barrier()
            vA = View(arA)
            w1h = vA.take(8 * 2048)
            load(w1h.rearrange("p (k n) -> p k n", k=ND),
                 w_ffn1.rearrange("(k p) n -> p k n", p=128)[:, :, hf * 2048:(hf + 1) * 2048],
                 sem="LW0")
            for ti in range(NCB):
                g0 = ti * TCB
                c.barrier()
                vC, vE = View(arC), View(arE)
                l2t = [vC.take(TCB) for _ in range(ND)]
                mid_t = [vE.take(TCB), vE.take(TCB)]
                for m in range(ND):
                    load(l2t[m], sp_ln2n[m * 128:(m + 1) * 128, g0:g0 + TCB], sem="LX0")
                lxv = c.v("LX0")
                wait("tensor", "LX0", lxv)
                wait("tensor", "LW0")
                evg = []
                for f in range(16):
                    fg = hf * 16 + f
                    bank = pb[f % 2]
                    if f >= 2:
                        wait("tensor", evg[f - 2][0], evg[f - 2][1])
                    for k in range(ND):
                        em("tensor", lambda e, f=f, k=k, bank=bank, w_=w1h, l_=l2t[k]: e.matmul(
                            bank[:, 0:TCB],
                            w_.rearrange("p (k n) -> p k n", k=ND)[:, k, f * 128:(f + 1) * 128],
                            l_[:], start=(k == 0), stop=(k == ND - 1)),
                            "MM" if k == ND - 1 else None)
                    wait("scalar", "MM")
                    if f >= 2:
                        wait("scalar", "SPm")
                    em("scalar", lambda e, f=f, fg=fg, bank=bank, mt=mid_t[f % 2]: e.activation(
                        mt[:], bank[:, 0:TCB], AF.Gelu_apprx_tanh,
                        bias=sbp["fb1"][:, fg:fg + 1]), "SC")
                    evg.append(("SC", c.v("SC")))
                    wait("scalar", "SC")
                    store(sp_mid[fg * 128:(fg + 1) * 128, g0:g0 + TCB],
                          mid_t[f % 2][:], sem="SPm")

    def b2c():
        for ti in range(NCB):
            g0 = ti * TCB
            c.barrier()
            vA, vB, vC, vD = View(arA), View(arB), View(arC), View(arD)
            w2h = [vA.take(NDT * 1024) for _ in range(1)]
            midt = [vB.take(TCB) for _ in range(NDT)]
            a1t = [vC.take(TCB) for _ in range(ND)]
            xt = [vC.take(TCB) for _ in range(ND)]
            x2t = [vD.take(TCB) for _ in range(ND)]
            for m in range(ND):
                load(a1t[m], sp_a1[m * 128:(m + 1) * 128, g0:g0 + TCB], sem="LX0")
                load(xt[m], x_in[m * 128:(m + 1) * 128, g0:g0 + TCB], sem="LX0")
            lax = c.v("LX0")
            for hf in range(2):
                load(w2h[0].rearrange("p (k m) -> p k m", m=1024),
                     w_ffn2.rearrange("(k p) m -> p k m", p=128)[:, hf * 16:(hf + 1) * 16, :],
                     sem="LW%d" % hf)
                for k in range(NDT):
                    load(midt[k], sp_mid[(hf * 16 + k) * 128:(hf * 16 + k + 1) * 128, g0:g0 + TCB],
                         sem="LW%d" % hf)
                lwv = c.v("LW%d" % hf)
                wait("tensor", "LW%d" % hf, lwv)
                if hf == 1:
                    pass
                for m in range(ND):
                    for k in range(NDT):
                        em("tensor", lambda e, m=m, k=k, hf=hf, w_=w2h[0], md_=midt[k]: e.matmul(
                            pb[m][:, 0:TCB],
                            w_.rearrange("p (k m) -> p k m", m=1024)[:, k, m * 128:(m + 1) * 128],
                            md_[:], start=(hf == 0 and k == 0),
                            stop=(hf == 1 and k == NDT - 1)),
                            "MM" if k == NDT - 1 else None)
                mmv = c.v("MM")
                if hf == 0:
                    wait("sync", "MM", mmv)
            wait("vector", "MM", mmv)
            wait("vector", "LX0", lax)
            for m in range(ND):
                em("vector", lambda e, m=m, x2_=x2t[m], a_=a1t[m]: e.scalar_tensor_tensor(
                    x2_[:], pb[m][:, 0:TCB], sbp["fb2"][:, m:m + 1], a_[:],
                    ALU.add, ALU.add), "VV")
            wait("vector", "VV")
            for m in range(ND):
                em("vector", lambda e, m=m, x2_=x2t[m], x_=xt[m]: e.scalar_tensor_tensor(
                    x2_[:], x2_[:], modv[:, 16 + m:17 + m], x_[:],
                    ALU.mult, ALU.add), "VV")
            x2v = c.v("VV")
            if ti == 0:
                wait("vector", "VV", x2v)
                for m in range(ND):
                    em("vector", lambda e, m=m, x2_=x2t[m]: e.tensor_copy(
                        edge[:, m:m + 1], x2_[:, 0:1]), "VV")
            if ti == NCB - 1:
                wait("vector", "VV", x2v)
                for m in range(ND):
                    em("vector", lambda e, m=m, x2_=x2t[m]: e.tensor_copy(
                        edge[:, 8 + m:9 + m], x2_[:, TCB - 1:TCB]), "VV")
            wait("scalar", "VV")
            for m in range(ND):
                store(sp_x2[m * 128:(m + 1) * 128, g0:g0 + TCB], x2t[m][:], sem="SPa")

    w1t_bf = arA[:].bitcast(BF16)
    m127 = {}

    def p4a():
        c.barrier()
        CM1, CM2, CSW, CMD = 3200, 3202, 3204, 3206
        for L in (2, 1):
            vD, vE = View(arD), View(arE)
            wt = [vD.take(1024), vD.take(1024)]
            bt_f = vD.take(512)
            bt = bt_f.bitcast(BF16)
            rcol = [vD.take(1), vD.take(1)]
            acc = vD.take(1)
            swb = vD.take(1)
            i32t_ = [vE.take(1024).bitcast(mybir.dt.int32), vE.take(1024).bitcast(mybir.dt.int32)]
            cl_ = [arF[:, 0:1024], arF[:, 1024:2048]]
            bt2_f = vE.take(512)
            bt2 = bt2_f.bitcast(BF16)
            m127[L] = edge[:, 15 + L:16 + L]

            def tile_src(u, L=L):
                if L == 2:
                    return w_bn2.rearrange("(k p) m -> p k m", p=128)[:, u, :]
                k, q = u // 4, u % 4
                return w_bn1.rearrange("(k p) n -> p k n", p=128)[:, k, q * 1024:(q + 1) * 1024]

            em("vector", lambda e, a=acc: e.memset(a, 0.0), "VV")
            prog = []
            for u in range(NF):
                lw = "LWt%d" % (u % 2)
                if u >= 2:
                    wait("sync", "VV", prog[u - 2])
                load(wt[u % 2], tile_src(u), sem=lw)
                wait("vector", lw)
                em("vector", lambda e, w=wt[u % 2], r=rcol[u % 2]: e.tensor_reduce(
                    r, w, AX.X, ALU.add, apply_absolute_value=True), "VV")
                wait("vector", "VV")
                em("vector", lambda e, r=rcol[u % 2], a=acc: e.tensor_add(a, a, r), "VV")
                prog.append(c.v("VV"))
            wait("tensor", "VV")
            em("tensor", lambda e, a=acc: e.matmul(
                pb[6][0:1, 0:1], ones_col, a, start=True, stop=True), "MM")
            wait("vector", "MM")
            cm = stat[0:1, CM1 + (L - 1):CM1 + L]
            cw = stat[0:1, CSW:CSW + 1]
            cd = stat[0:1, CMD + (L - 1):CMD + L]
            em("vector", lambda e, cm=cm: e.tensor_copy(cm, pb[6][0:1, 0:1]), "VV")
            wait("vector", "VV")
            em("vector", lambda e, cm=cm: e.tensor_single_scalar(
                cm, cm, 1.0 / 4194304.0, ALU.mult), "VV")
            wait("vector", "VV")
            em("vector", lambda e, cm=cm: e.tensor_single_scalar(cm, cm, 1e-5, ALU.max), "VV")
            wait("vector", "VV")
            em("vector", lambda e, cm=cm, cw=cw: e.reciprocal(cw, cm), "VV")
            wait("vector", "VV")
            em("vector", lambda e, cm=cm, cd=cd: e.tensor_single_scalar(
                cd, cm, 1.0 / 127.0, ALU.mult), "VV")
            wait("scalar", "VV")
            bcast(swb, cw, 1)
            bcast(m127[L], cd, 1)
            wait("vector", "LB")
            prog = []
            for u in range(NF):
                lw = "LWt%d" % (u % 2)
                par = u % 2
                i32t = i32t_[par]
                cl = cl_[par]
                btu = (bt, bt2)[par]
                if u >= 2:
                    wait("sync", "VV", prog[u - 2][0])
                load(wt[par], tile_src(u), sem=lw)
                wait("vector", lw)
                if u >= 2:
                    wait("vector", "SC", prog[u - 2][1])
                em("vector", lambda e, w=wt[par], swb=swb, cl=cl: e.tensor_scalar(
                    cl, w, swb, None, ALU.mult), "VV")
                qv = c.v("VV")
                wait("scalar", "VV", qv)
                em("scalar", lambda e, i32t=i32t, cl=cl: e.copy(i32t, cl), "SC")
                i32v = c.v("SC")
                wait("scalar", "SC", i32v)
                em("scalar", lambda e, i32t=i32t, cl=cl: e.copy(cl, i32t), "SC")
                rbv = c.v("SC")
                wait("vector", "SC", rbv)
                em("vector", lambda e, cl=cl: e.tensor_single_scalar(cl, cl, 1.0, ALU.min), "VV")
                wait("vector", "VV")
                em("vector", lambda e, cl=cl: e.tensor_single_scalar(cl, cl, -1.0, ALU.max), "VV")
                wait("vector", "VV")
                if L == 1:
                    em("vector", lambda e, u=u, cl=cl: e.tensor_copy(
                        w1t_bf[:, u * 1024:(u + 1) * 1024], cl), "VV")
                    cpv = c.v("VV")
                else:
                    if u >= 2:
                        wait("vector", "SPw", prog[u - 2][2])
                    em("vector", lambda e, b=btu, cl=cl: e.tensor_copy(b, cl), "VV")
                    cpv = c.v("VV")
                    wait("scalar", "VV", cpv)
                    store(sp_w2t[:, u * 1024:(u + 1) * 1024], btu, sem="SPw")
                prog.append((cpv, i32v, c.v("SPw") if L == 2 else 0))
            c.barrier()
        if DEBUG:
            for j in range(8):
                store(dbg_w1t[:, j * 4096:(j + 1) * 4096],
                      w1t_bf[:, j * 4096:(j + 1) * 4096], sem="SPw")
            c.barrier()

    bn_tails = arD[:, 3900:3964]

    def p4c():
        x2src = sp_x2 if x2_in is None else x2_in
        wait("vector", "VV")
        em("vector", lambda e: e.memset(bn_tails[:], 0.0), "VV")
        wait("vector", "VV")
        em("vector", lambda e: e.tensor_copy(
            bn_tails.rearrange("p (f two) -> p f two", two=2)[:, :, 1], mhL[:]), "VV")
        for ci in range(NCB):
            g0 = ci * TCB
            c.barrier()
            vB, vC, vD, vE = View(arB), View(arC), View(arD), View(arE)
            x2t = [vB.take(TCB) for _ in range(ND)]
            hbt = [vB.take(TCB) for _ in range(ND)]
            qx_f = vC.take(2048)
            qx = qx_f.bitcast(BF16)
            sq = [vC.take(TCB), vC.take(TCB)]
            rep_a, rep_b = vC.take(TCB), vC.take(TCB)
            rmax = vC.take(TCB)
            tmp = vC.take(TCB)
            r1 = vC.take(TCB)
            sx1 = vC.take(TCB)
            ds1 = vC.take(TCB)
            mid1 = [vC.take(TCB + 2), vC.take(TCB + 2)]
            oc = [vD.take(TCB), vD.take(TCB)]
            ms = [vD.take(TCB), vD.take(TCB)]
            rmax2w = vD.take(TCB)
            sqr = [vD.take(TCB), vD.take(TCB)]
            acl4 = vD.take(4)
            i32t = [vE.take(TCB).bitcast(mybir.dt.int32), vE.take(TCB).bitcast(mybir.dt.int32)]
            for i in range(ND):
                load(x2t[i], x2src[i * 128:(i + 1) * 128, g0:g0 + TCB], sem="LX0")
            lxv = c.v("LX0")
            for eng in ("tensor", "scalar", "vector"):
                wait(eng, "LX0", lxv)
            em("gpsimd", lambda e, r=rmax2w: e.memset(r, 0.0), "GP")
            gp_init = c.v("GP")
            emit_ln(x2t, hbt, TCB,
                    [modv[:, 32 + i:33 + i] for i in range(ND)],
                    [modv[:, 24 + i:25 + i] for i in range(ND)],
                    pb[7], sq, rep_a, rep_b)
            wait("vector", "VV")
            em("vector", lambda e: e.scalar_tensor_tensor(
                rmax[:], hbt[0][:], consts[:, 1:2], hbt[0][:], ALU.mult, ALU.max), "VV")
            for i in range(1, ND):
                wait("vector", "VV")
                em("vector", lambda e, i=i: e.scalar_tensor_tensor(
                    tmp[:], hbt[i][:], consts[:, 1:2], hbt[i][:], ALU.mult, ALU.max), "VV")
                wait("vector", "VV")
                em("vector", lambda e: e.tensor_max(rmax[:], rmax[:], tmp[:]), "VV")
            wait("vector", "VV")
            em("vector", lambda e: e.tensor_copy(tmp[0:64, :], rmax[64:128, :]), "VV")
            wait("vector", "VV")
            em("vector", lambda e: e.tensor_max(rmax[0:64, :], rmax[0:64, :], tmp[0:64, :]), "VV")
            wait("vector", "VV")
            em("vector", lambda e: e.tensor_copy(tmp[0:32, :], rmax[32:64, :]), "VV")
            wait("vector", "VV")
            em("vector", lambda e: e.tensor_max(rmax[0:32, :], rmax[0:32, :], tmp[0:32, :]), "VV")
            vfold = c.v("VV")
            wait("tensor", "VV", vfold)
            for blk in range(4):
                em("tensor", lambda e, blk=blk: e.transpose(
                    pb[6][:, blk * 32:(blk + 1) * 32],
                    rmax[0:32, blk * 128:(blk + 1) * 128], ident[0:32, 0:32]), "MM")
            wait("vector", "MM")
            for blk in range(4):
                em("vector", lambda e, blk=blk: e.tensor_reduce(
                    acl4[:, blk:blk + 1], pb[6][:, blk * 32:(blk + 1) * 32], AX.X, ALU.max),
                    "VV" if blk == 3 else None)
            wait("scalar", "VV")
            ri = _ring[0] % 16
            _ring[0] += 1
            for blk in range(4):
                store(sp_row[ri:ri + 1, blk * 128:(blk + 1) * 128], acl4[:, blk:blk + 1], sem="SR")
            wait("sync", "SR")
            c.dsem("LB")
            c.em("sync", lambda e, d=r1, ri=ri: e.dma_start(
                out=d, in_=sp_row[ri:ri + 1, 0:TCB].partition_broadcast(128).squeeze(1)), "LB", 16)
            wait("vector", "LB")
            em("vector", lambda e: e.tensor_single_scalar(r1[:], r1[:], 1e-5, ALU.max), "VV")
            wait("vector", "VV")
            em("vector", lambda e: e.reciprocal(sx1[:], r1[:]), "VV")
            wait("vector", "VV")
            em("vector", lambda e: e.tensor_single_scalar(sx1[:], sx1[:], 127.0, ALU.mult), "VV")
            wait("vector", "VV")
            em("vector", lambda e: e.tensor_scalar(ds1[:], r1[:], m127[1], None, ALU.mult), "VV")
            qprog = []
            for k in range(ND):
                par = k % 2
                if k >= 2:
                    wait("vector", "SC", qprog[k - 2])
                wait("vector", "VV")
                em("vector", lambda e, k=k, par=par: e.tensor_mul(sq[par][:], hbt[k][:], sx1[:]), "VV")
                wait("scalar", "VV")
                em("scalar", lambda e, par=par: e.copy(i32t[par], sq[par][:]), "SC")
                wait("scalar", "SC")
                em("scalar", lambda e, k=k, par=par: e.copy(
                    qx[:, k * TCB:(k + 1) * TCB], i32t[par]), "SC")
                qprog.append(c.v("SC"))
            qxv = c.v("SC")
            wait("tensor", "SC", qxv)
            wait("gpsimd", "GP", gp_init)
            evv, scv, gpv, mxv = {}, {}, {}, {}
            for f in range(NF):
                par = f % 2
                bank = pb[par]
                if f >= 2:
                    wait("tensor", "VV", evv[f - 2])
                for k in range(ND):
                    em("tensor", lambda e, f=f, k=k, bank=bank: e.matmul(
                        bank[:, 0:TCB],
                        w1t_bf[:, k * 4096 + f * 128:k * 4096 + f * 128 + 128],
                        qx[:, k * TCB:(k + 1) * TCB],
                        start=(k == 0), stop=(k == ND - 1)),
                        "MM" if k == ND - 1 else None)
                mmv = c.v("MM")
                wait("vector", "MM", mmv)
                if f >= 2:
                    wait("vector", "SC", scv[f - 2])
                em("vector", lambda e, par=par, bank=bank, m=mid1[par]: e.tensor_tensor(
                    m[:, 2:TCB + 2], bank[:, 0:TCB], ds1[:], ALU.mult))
                em("vector", lambda e, par=par, f=f, m=mid1[par]: e.tensor_copy(
                    m[:, 0:2], bn_tails[:, f * 2:f * 2 + 2]), "VV")
                evv[f] = c.v("VV")
                wait("scalar", "VV", evv[f])
                if f >= 2:
                    wait("scalar", "GP", gpv[f - 2])
                em("scalar", lambda e, par=par, f=f, m=mid1[par], o=oc[par]: e.activation(
                    o[:], m[:, 0:TCB], AF.Copy, scale=sbp["bconv_w"][:, f * 3:f * 3 + 1]), "SC")
                t0v = c.v("SC")
                wait("vector", "SC", t0v)
                em("vector", lambda e, par=par, f=f, m=mid1[par], o=oc[par]: e.scalar_tensor_tensor(
                    o[:], m[:, 1:TCB + 1], sbp["bconv_w"][:, f * 3 + 1:f * 3 + 2], o[:],
                    ALU.mult, ALU.add))
                em("vector", lambda e, par=par, f=f, m=mid1[par], o=oc[par]: e.scalar_tensor_tensor(
                    o[:], m[:, 2:TCB + 2], sbp["bconv_w"][:, f * 3 + 2:f * 3 + 3], o[:],
                    ALU.mult, ALU.add))
                em("vector", lambda e, par=par, f=f, m=mid1[par]: e.tensor_copy(
                    bn_tails[:, f * 2:f * 2 + 2], m[:, TCB:TCB + 2]), "VV")
                cvv = c.v("VV")
                wait("scalar", "VV", cvv)
                em("scalar", lambda e, par=par, f=f, o=oc[par], s=ms[par]: e.activation(
                    s[:], o[:], AF.Silu, bias=sbp["bconv_b"][:, f:f + 1]), "SC")
                scv[f] = c.v("SC")
                wait("scalar", "SC", scv[f])
                if ci == 0:
                    store(sp_bmid2[f * 128:(f + 1) * 128, 0:TCB - 1],
                          ms[par][:, 1:TCB], sem="SPm")
                else:
                    store(sp_bmid2[f * 128:(f + 1) * 128, g0 - 1:g0 + TCB - 1],
                          ms[par][:], sem="SPm")
                wait("gpsimd", "SC", scv[f])
                if f >= 2:
                    wait("gpsimd", "VV", mxv[f - 2])
                em("gpsimd", lambda e, par=par: e.tensor_tensor(
                    sqr[par][:], ms[par][:], ms[par][:], ALU.mult), "GP")
                gpv[f] = c.v("GP")
                wait("vector", "GP", gpv[f])
                em("vector", lambda e, par=par: e.tensor_max(
                    rmax2w[:], rmax2w[:], sqr[par][:]), "VV")
                mxv[f] = c.v("VV")
            wait("vector", "VV", mxv[NF - 1])
            em("vector", lambda e: e.tensor_copy(tmp[0:64, :], rmax2w[64:128, :]), "VV")
            wait("vector", "VV")
            em("vector", lambda e: e.tensor_max(rmax2w[0:64, :], rmax2w[0:64, :], tmp[0:64, :]), "VV")
            wait("vector", "VV")
            em("vector", lambda e: e.tensor_copy(tmp[0:32, :], rmax2w[32:64, :]), "VV")
            wait("vector", "VV")
            em("vector", lambda e: e.tensor_max(rmax2w[0:32, :], rmax2w[0:32, :], tmp[0:32, :]), "VV")
            vfold2 = c.v("VV")
            wait("tensor", "VV", vfold2)
            for blk in range(4):
                em("tensor", lambda e, blk=blk: e.transpose(
                    pb[6][:, blk * 32:(blk + 1) * 32],
                    rmax2w[0:32, blk * 128:(blk + 1) * 128], ident[0:32, 0:32]), "MM")
            wait("vector", "MM")
            for blk in range(4):
                em("vector", lambda e, blk=blk: e.tensor_reduce(
                    acl4[:, blk:blk + 1], pb[6][:, blk * 32:(blk + 1) * 32], AX.X, ALU.max),
                    "VV" if blk == 3 else None)
            wait("scalar", "VV")
            for blk in range(4):
                if ci == 0 and blk == 0:
                    store(sp_rm2[0:1, 0:127], acl4[1:128, 0:1], sem="SR")
                else:
                    store(sp_rm2[0:1, g0 - 1 + blk * 128:g0 - 1 + (blk + 1) * 128],
                          acl4[:, blk:blk + 1], sem="SR")

    def p4d():
        c.barrier()
        vE = View(arE)
        oc32 = vE.take(32)
        tm32 = vE.take(32)
        es32 = vE.take(32)
        mcol = vE.take(1)
        fcol = vE.take(1)
        cell = vE.take(2)
        w3 = sbp["bconv_w"].rearrange("p (f th) -> p f th", th=3)
        t2 = bn_tails.rearrange("p (f two) -> p f two", two=2)
        wait("vector", "VV")
        em("vector", lambda e: e.tensor_tensor(oc32[:], t2[:, :, 0], w3[:, :, 0], ALU.mult), "VV")
        wait("vector", "VV")
        em("vector", lambda e: e.tensor_tensor(tm32[:], t2[:, :, 1], w3[:, :, 1], ALU.mult), "VV")
        wait("vector", "VV")
        em("vector", lambda e: e.tensor_add(oc32[:], oc32[:], tm32[:]), "VV")
        wait("vector", "VV")
        em("vector", lambda e: e.tensor_tensor(tm32[:], mhR[:], w3[:, :, 2], ALU.mult), "VV")
        wait("vector", "VV")
        em("vector", lambda e: e.tensor_add(oc32[:], oc32[:], tm32[:]), "VV")
        wait("vector", "VV")
        em("vector", lambda e: e.tensor_add(oc32[:], oc32[:], sbp["bconv_b"][:, 0:32]), "VV")
        wait("scalar", "VV")
        em("scalar", lambda e: e.activation(es32[:], oc32[:], AF.Silu), "SC")
        wait("scalar", "SC")
        c.em("scalar", lambda e: e.dma_start(
            out=sp_bmid2.rearrange("(f p) t -> p f t", p=128)[:, :, T - 1:T],
            in_=es32[:].unsqueeze(2), allow_slow_non_contiguous=True), c.dsem("SPm"), 16)
        wait("vector", "SC")
        em("vector", lambda e: e.tensor_reduce(mcol, es32[:], AX.X, ALU.max,
                                               apply_absolute_value=True), "VV")
        wait("vector", "VV")
        em("vector", lambda e: e.tensor_copy(fcol[0:64, :], mcol[64:128, :]), "VV")
        wait("vector", "VV")
        em("vector", lambda e: e.tensor_max(mcol[0:64, :], mcol[0:64, :], fcol[0:64, :]), "VV")
        wait("vector", "VV")
        em("vector", lambda e: e.tensor_copy(fcol[0:32, :], mcol[32:64, :]), "VV")
        wait("vector", "VV")
        em("vector", lambda e: e.tensor_max(mcol[0:32, :], mcol[0:32, :], fcol[0:32, :]), "VV")
        mf = c.v("VV")
        wait("tensor", "VV", mf)
        em("tensor", lambda e: e.transpose(pb[6][0:1, 0:32], mcol[0:32, 0:1], ident[0:32, 0:32]), "MM")
        wait("vector", "MM")
        em("vector", lambda e: e.tensor_reduce(cell[0:1, 0:1], pb[6][0:1, 0:32], AX.X, ALU.max), "VV")
        wait("vector", "VV")
        em("vector", lambda e: e.tensor_mul(cell[0:1, 1:2], cell[0:1, 0:1], cell[0:1, 0:1]), "VV")
        wait("scalar", "VV")
        store(sp_rm2[0:1, T - 1:T], cell[0:1, 1:2], sem="SR")
        c.barrier()
        w2t_res = arA[:].bitcast(BF16)
        load(w2t_res[:, 0:NF * 1024], sp_w2t[:], sem="LW0")
        x2src = sp_x2 if x2_in is None else x2_in
        for ci in range(NCB):
            g0 = ci * TCB
            c.barrier()
            vB, vC, vD, vE = View(arB), View(arC), View(arD), View(arE)
            mt = [ (vB if k < 16 else vC).take(TCB) for k in range(NF) ]
            r2 = vD.take(TCB)
            sx2 = vD.take(TCB)
            ds2 = vD.take(TCB)
            qt = [vD.take(TCB), vD.take(TCB)]
            yt = vD.take(TCB)
            i32t = [vE.take(TCB).bitcast(mybir.dt.int32), vE.take(TCB).bitcast(mybir.dt.int32)]
            x2p = [vE.take(TCB), vE.take(TCB)]
            for k in range(NF):
                load(mt[k], sp_bmid2[k * 128:(k + 1) * 128, g0:g0 + TCB], sem="LX0")
            lxv = c.v("LX0")
            c.dsem("LB")
            c.em("sync", lambda e, d=r2, g0=g0: e.dma_start(
                out=d, in_=sp_rm2[0:1, g0:g0 + TCB].partition_broadcast(128).squeeze(1)), "LB", 16)
            wait("scalar", "LB")
            em("scalar", lambda e: e.activation(r2[:], r2[:], AF.Sqrt), "SC")
            wait("vector", "SC")
            em("vector", lambda e: e.tensor_single_scalar(r2[:], r2[:], 1e-5, ALU.max), "VV")
            wait("vector", "VV")
            em("vector", lambda e: e.reciprocal(sx2[:], r2[:]), "VV")
            wait("vector", "VV")
            em("vector", lambda e: e.tensor_single_scalar(sx2[:], sx2[:], 127.0, ALU.mult), "VV")
            wait("vector", "VV")
            em("vector", lambda e: e.tensor_scalar(ds2[:], r2[:], m127[2], None, ALU.mult), "VV")
            wait("vector", "LX0", lxv)
            wait("tensor", "LW0")
            qv = {}
            for k in range(NF):
                par = k % 2
                if k >= 2:
                    wait("vector", "SC", qv[k - 2])
                wait("vector", "VV")
                em("vector", lambda e, k=k, par=par: e.tensor_mul(qt[par][:], mt[k][:], sx2[:]), "VV")
                wait("scalar", "VV")
                em("scalar", lambda e, par=par: e.copy(i32t[par], qt[par][:]), "SC")
                wait("scalar", "SC")
                arena = arB if k < 16 else arC
                boff = (mt[k].offset_cols if False else None)
                em("scalar", lambda e, k=k, par=par, arena=arena: e.copy(
                    arena[:].bitcast(BF16)[:, (k % 16) * 1024:(k % 16) * 1024 + TCB],
                    i32t[par]), "SC")
                qv[k] = c.v("SC")
                wait("tensor", "SC", qv[k])
                for m in range(ND):
                    em("tensor", lambda e, k=k, m=m, arena=arena: e.matmul(
                        pb[m][:, 0:TCB],
                        w2t_res[:, k * 1024 + m * 128:k * 1024 + m * 128 + 128],
                        arena[:].bitcast(BF16)[:, (k % 16) * 1024:(k % 16) * 1024 + TCB],
                        start=(k == 0), stop=(k == NF - 1)),
                        "MM" if (k == NF - 1) else None)
            mmv = c.v("MM")
            wait("vector", "MM", mmv)
            evs = []
            for m in range(ND):
                lxe = "LXe%d" % (m % 2)
                c.dsem(lxe)
                if m >= 2:
                    wait("sync", "VV", evs[m - 2])
                load(x2p[m % 2], x2src[m * 128:(m + 1) * 128, g0:g0 + TCB], sem=lxe)
                lev = c.v(lxe)
                em("vector", lambda e, m=m: e.tensor_tensor(
                    yt[:], pb[m][:, 0:TCB], ds2[:], ALU.mult), "VV")
                wait("vector", lxe, lev)
                wait("vector", "VV")
                em("vector", lambda e, m=m, par=m % 2: e.scalar_tensor_tensor(
                    yt[:], yt[:], modv[:, 40 + m:41 + m], x2p[par][:],
                    ALU.mult, ALU.add), "VV")
                evs.append(c.v("VV"))
                wait("scalar", "VV")
                store(out_ext[m * 128:(m + 1) * 128, g0:g0 + TCB], yt[:], sem="SO")
                wait("vector", "SO")
        c.barrier()

    mhL = arD[:, 4096:4128]
    mhR = arD[:, 4128:4160]

    def p4b():
        CMD = 3206
        c.barrier()
        if edge_in is not None:
            load(edge[:, 0:16], edge_in, sem="LE")
            wait("gpsimd", "LE")
        c.dsem("GD")
        em("gpsimd", lambda e: e.dma_start(
            out=ag2_in.rearrange("o (p n) -> (o p) n", p=128), in_=edge[:, 0:16]), "GD", 16)
        wait("gpsimd", "GD")
        groups = [[0, 1], [2, 3], [4, 5], [6, 7]]
        em("gpsimd", lambda e, groups=groups: e.collective_compute(
            "AllGather", ALU.bypass, replica_groups=groups,
            ins=[ag2_in.opt()], outs=[ag2_out.opt()]), "CC")
        wait("sync", "CC")
        vE = View(arE)
        xv = [vE.take(8), vE.take(8)]
        sq8 = vE.take(8)
        hb8 = [vE.take(8), vE.take(8)]
        mcol = vE.take(1)
        fcol = vE.take(1)
        rowc = vE.take(4)
        bc = [vE.take(1) for _ in range(6)]
        qlr_f = vE.take(8)
        qlr = qlr_f.bitcast(BF16)
        i8 = vE.take(16).bitcast(mybir.dt.int32)
        c.dsem("LH2")
        c.em("sync", lambda e: e.dma_start(
            out=xv[0], in_=ag2_out[0:1, :].rearrange("o (p n) -> (o p) n", p=128)[:, 8:16]), "LH2", 16)
        c.em("sync", lambda e: e.dma_start(
            out=xv[1], in_=ag2_out[1:2, :].rearrange("o (p n) -> (o p) n", p=128)[:, 0:8]), "LH2", 16)
        lh = c.v("LH2")
        for eng in ("tensor", "scalar", "vector"):
            wait(eng, "LH2", lh)
        for v in range(2):
            x8 = xv[v]
            em("tensor", lambda e, x8=x8: e.matmul(pb[6][0:1, 0:8], ones_col, x8,
                                                   start=True, stop=True), "MM")
            wait("scalar", "MM")
            em("scalar", lambda e, x8=x8: e.activation(sq8[:], x8, AF.Square), "SC")
            wait("tensor", "SC")
            wait("vector", "MM")
            em("vector", lambda e: e.tensor_reduce(rowc[0:1, 0:1], pb[6][0:1, 0:8], AX.X, ALU.add), "VV")
            wait("tensor", "VV")
            em("tensor", lambda e: e.matmul(pb[6][0:1, 8:16], ones_col, sq8[:],
                                            start=True, stop=True), "MM")
            wait("vector", "MM")
            em("vector", lambda e: e.tensor_reduce(rowc[0:1, 1:2], pb[6][0:1, 8:16], AX.X, ALU.add), "VV")
            wait("vector", "VV")
            em("vector", lambda e: e.tensor_single_scalar(rowc[0:1, 0:2], rowc[0:1, 0:2], 1.0 / D, ALU.mult), "VV")
            wait("vector", "VV")
            em("vector", lambda e: e.tensor_mul(rowc[0:1, 2:3], rowc[0:1, 0:1], rowc[0:1, 0:1]), "VV")
            wait("vector", "VV")
            em("vector", lambda e: e.tensor_sub(rowc[0:1, 1:2], rowc[0:1, 1:2], rowc[0:1, 2:3]), "VV")
            wait("scalar", "VV")
            em("scalar", lambda e: e.activation(rowc[0:1, 1:2], rowc[0:1, 1:2], AF.Sqrt, bias=1e-5), "SC")
            wait("vector", "SC")
            em("vector", lambda e: e.reciprocal(rowc[0:1, 1:2], rowc[0:1, 1:2]), "VV")
            wait("scalar", "VV")
            bcast(bc[0], rowc[0:1, 0:1], 1)
            bcast(bc[1], rowc[0:1, 1:2], 1)
            wait("vector", "LB")
            hb = hb8[v]
            em("vector", lambda e, x8=x8, hb=hb: e.tensor_scalar(hb, x8, bc[0], None, ALU.subtract), "VV")
            wait("vector", "VV")
            em("vector", lambda e, hb=hb: e.tensor_scalar(hb, hb, bc[1], None, ALU.mult), "VV")
            wait("vector", "VV")
            em("vector", lambda e, hb=hb: e.tensor_mul(hb, hb, modv[:, 32:40]), "VV")
            wait("vector", "VV")
            em("vector", lambda e, hb=hb: e.tensor_add(hb, hb, modv[:, 24:32]), "VV")
            wait("vector", "VV")
            if DEBUG:
                wait("scalar", "VV")
                store(sp_hh[:, 64 + v * 8:64 + (v + 1) * 8], hb, sem="SPh")
                wait("vector", "SPh")
            em("vector", lambda e, hb=hb: e.tensor_reduce(mcol, hb, AX.X, ALU.max,
                                                          apply_absolute_value=True), "VV")
            wait("vector", "VV")
            em("vector", lambda e: e.tensor_copy(fcol[0:64, :], mcol[64:128, :]), "VV")
            wait("vector", "VV")
            em("vector", lambda e: e.tensor_tensor(mcol[0:64, :], mcol[0:64, :], fcol[0:64, :], ALU.max), "VV")
            wait("vector", "VV")
            em("vector", lambda e: e.tensor_copy(fcol[0:32, :], mcol[32:64, :]), "VV")
            wait("vector", "VV")
            em("vector", lambda e: e.tensor_tensor(mcol[0:32, :], mcol[0:32, :], fcol[0:32, :], ALU.max), "VV")
            mfold = c.v("VV")
            wait("tensor", "VV", mfold)
            em("tensor", lambda e: e.transpose(pb[6][0:1, 16:48], mcol[0:32, 0:1], ident[0:32, 0:32]), "MM")
            wait("vector", "MM")
            em("vector", lambda e: e.tensor_reduce(rowc[0:1, 2:3], pb[6][0:1, 16:48], AX.X, ALU.max), "VV")
            wait("vector", "VV")
            em("vector", lambda e: e.tensor_single_scalar(rowc[0:1, 2:3], rowc[0:1, 2:3], 1e-5, ALU.max), "VV")
            wait("vector", "VV")
            em("vector", lambda e: e.tensor_mul(rowc[0:1, 3:4], rowc[0:1, 2:3], stat[0:1, CMD:CMD + 1]), "VV")
            wait("vector", "VV")
            em("vector", lambda e: e.reciprocal(rowc[0:1, 2:3], rowc[0:1, 2:3]), "VV")
            wait("vector", "VV")
            em("vector", lambda e: e.tensor_single_scalar(rowc[0:1, 2:3], rowc[0:1, 2:3], 127.0, ALU.mult), "VV")
            wait("scalar", "VV")
            bcast(bc[2 + v], rowc[0:1, 2:3], 1)
            bcast(bc[4 + v], rowc[0:1, 3:4], 1)
            wait("vector", "LB")
            em("vector", lambda e, hb=hb, v=v: e.tensor_scalar(hb, hb, bc[2 + v], None, ALU.mult), "VV")
            wait("scalar", "VV")
            em("scalar", lambda e, v=v: e.copy(i8[:, v * 8:(v + 1) * 8], hb8[v]), "SC")
            wait("scalar", "SC")
            em("scalar", lambda e, v=v: e.copy(
                qlr.rearrange("p (k two) -> p k two", two=2)[:, :, v], i8[:, v * 8:(v + 1) * 8]), "SC")
            if DEBUG:
                wait("scalar", "SC")
                store(sp_stat[0:1, 3000 + v * 4:3004 + v * 4], rowc[0:1, 0:4], sem="SPh")
                store(sp_hh[:, 96 + v * 8:96 + (v + 1) * 8], hb, sem="SPh")
        qv_ = c.v("SC")
        if DEBUG:
            wait("vector", "SC", qv_)
            em("vector", lambda e: e.tensor_copy(sq8[:], i8[:, 0:8]), "VV")
            em("vector", lambda e: e.tensor_copy(hb8[0][:], i8[:, 8:16]), "VV")
            wait("scalar", "VV")
            store(sp_hh[:, 128:136], sq8[:], sem="SPh")
            store(sp_hh[:, 136:144], hb8[0][:], sem="SPh")
            wait("vector", "SPh")
            wait("tensor", "SPh")
        wait("tensor", "SC", qv_)
        evg = []
        for f in range(NF):
            bank = pb[4 + (f % 2)]
            if f >= 2:
                wait("tensor", evg[f - 2][0], evg[f - 2][1])
            for k in range(ND):
                em("tensor", lambda e, f=f, k=k, bank=bank: e.matmul(
                    bank[:, 0:2],
                    w1t_bf[:, (k * 4096 + f * 128) * 1:(k * 4096 + f * 128) + 128],
                    qlr[:, k * 2:(k + 1) * 2], start=(k == 0), stop=(k == ND - 1)),
                    "MM" if k == ND - 1 else None)
            wait("vector", "MM")
            em("vector", lambda e, f=f, bank=bank: e.tensor_scalar(
                mhL[:, f:f + 1], bank[:, 0:1], bc[4], None, ALU.mult))
            em("vector", lambda e, f=f, bank=bank: e.tensor_scalar(
                mhR[:, f:f + 1], bank[:, 1:2], bc[5], None, ALU.mult), "VV")
            evg.append(("VV", c.v("VV")))
        wait("vector", "VV")
        em("vector", lambda e: e.tensor_scalar(mhL[:], mhL[:], consts[:, 3:4], None, ALU.mult), "VV")
        wait("vector", "VV")
        em("vector", lambda e: e.tensor_scalar(mhR[:], mhR[:], consts[:, 5:6], None, ALU.mult), "VV")
        if DEBUG:
            c.barrier()
            store(sp_cl[:, 0:32], mhL[:])
            store(sp_cl[:, 32:64], mhR[:])
            c.barrier()

    if "P0" in PHASES:
        p0()
    if "A12" in PHASES:
        a12()
    if "A4" in PHASES:
        a4()
    if "B1" in PHASES:
        b1()
    if "B2" in PHASES:
        b2a()
        b2b()
        b2c()
    if "P4A" in PHASES:
        p4a()
    if "P4B" in PHASES:
        p4b()
    if "P4C" in PHASES:
        p4c()
    if "P4D" in PHASES:
        p4d()
    else:
        c.dsem("GD")
        em("gpsimd", lambda e: e.dma_start(out=out_ext[0:128, 0:8], in_=modv[:, 0:8]), "GD", 16)
        wait("gpsimd", "GD")

    c.barrier()

    with nc.Block() as block:
        @block.sync
        def _(eng):
            for f in c.ops["sync"]:
                f(eng)

        @block.vector
        def _(eng):
            for f in c.ops["vector"]:
                f(eng)

        @block.scalar
        def _(eng):
            for f in c.ops["scalar"]:
                f(eng)

        @block.tensor
        def _(eng):
            for f in c.ops["tensor"]:
                f(eng)

        @block.gpsimd
        def _(eng):
            for f in c.ops["gpsimd"]:
                f(eng)

    n_inst = sum(len(v) for v in c.ops.values())
    print("planned ops:", n_inst)
    return nc


def make_inmaps(inp, cores=range(8)):
    def fm(a):
        return np.ascontiguousarray(np.asarray(a, np.float32).T)

    def dmaj(v, nt):
        return np.ascontiguousarray(np.asarray(v, np.float32).reshape(nt, 128).T)

    maps = []
    for cid in cores:
        b, h = cid // 2, cid % 2
        t0 = h * T
        x = np.asarray(inp["L0_attn_x"][b], np.float32)
        m = {
            "x": fm(x[t0:t0 + T]),
            "xhalo": fm(x[t0 - 3:t0]) if h == 1 else np.zeros((D, 3), np.float32),
            "w_inproj": np.asarray(inp["L0_attn_in_proj_w"], np.float32),
            "w_xproj": np.asarray(inp["L0_attn_x_proj_w"], np.float32),
            "w_dtproj": np.asarray(inp["L0_attn_dt_proj_w"], np.float32),
            "w_outproj": np.asarray(inp["L0_attn_out_proj_w"], np.float32),
            "w_ffn1": np.asarray(inp["L0_attn_ffn_w1"], np.float32),
            "w_ffn2": np.asarray(inp["L0_attn_ffn_w2"], np.float32),
            "w_bn1": np.ascontiguousarray(np.asarray(inp["L0_ffn_w1"], np.float32).T),
            "w_bn2": np.ascontiguousarray(np.asarray(inp["L0_ffn_w2"], np.float32).T),
            "w_mod": np.asarray(inp["L0_wmod"], np.float32),
            "c_vec": dmaj(inp["L0_c"][b], ND),
            "ln1g": dmaj(inp["L0_attn_ln1_g"], ND),
            "ln1b": dmaj(inp["L0_attn_ln1_b"], ND),
            "ln2g": dmaj(inp["L0_attn_ln2_g"], ND),
            "ln2b": dmaj(inp["L0_attn_ln2_b"], ND),
            "mconv_w": np.ascontiguousarray(
                np.asarray(inp["L0_attn_conv_w"], np.float32).reshape(NDT, 128, 4).transpose(1, 0, 2).reshape(128, NDT * 4)),
            "mconv_b": dmaj(inp["L0_attn_conv_b"], NDT),
            "dt_bias": dmaj(inp["L0_attn_dt_proj_b"], NDT),
            "a_log": np.ascontiguousarray(
                np.asarray(inp["L0_attn_A_log"], np.float32).reshape(NDT, 128, DS).transpose(1, 0, 2).reshape(128, NDT * DS)),
            "d_vec": dmaj(inp["L0_attn_D"], NDT),
            "bconv_w": np.ascontiguousarray(
                np.asarray(inp["L0_ffn_conv_w"], np.float32).reshape(NF, 128, 3).transpose(1, 0, 2).reshape(128, NF * 3)),
            "bconv_b": dmaj(inp["L0_ffn_conv_b"], NF),
            "fb1": dmaj(inp["L0_attn_ffn_b1"], NF),
            "fb2": dmaj(inp["L0_attn_ffn_b2"], ND),
            "hmask": np.full((128, 3), 1.0 if h == 1 else 0.0, np.float32),
            "consts": np.tile(np.array([[1.0, -1.0, 1e-5, float(h),
                                         float(h), 1.0 - float(h), 0.0, 0.0]], np.float32), (128, 1)),
            "consts2": np.concatenate([np.full((128, 8), 1.0, np.float32),
                                       np.full((128, 8), 0.25, np.float32)], axis=1),
            "ident_in": np.eye(128, dtype=np.float32),
        }
        maps.append(m)
    return maps




_NC_CACHE = [None]


def kernel(**inputs):
    if _NC_CACHE[0] is None:
        _NC_CACHE[0] = build_nc()
    nc = _NC_CACHE[0]
    maps = make_inmaps(inputs, cores=range(8))
    res = run_bass_kernel_spmd(nc, maps, core_ids=list(range(8)))
    out = np.empty((4, 8192, 1024), np.float32)
    for cid in range(8):
        b, h = cid // 2, cid % 2
        out[b, h * T:(h + 1) * T, :] = np.asarray(res.results[cid]["out"]).T
    return out
